# version 12
# baseline (speedup 1.0000x reference)
import numpy as np
import ml_dtypes
from contextlib import ExitStack

import concourse.bass as bass
import concourse.mybir as mybir
from concourse.bass_utils import run_bass_kernel_spmd

F32 = mybir.dt.float32
BF16 = mybir.dt.bfloat16
AXX = mybir.AxisListType.X
OP = mybir.AluOpType
AF = mybir.ActivationFunctionType
BF = ml_dtypes.bfloat16

S, D, NH, DK, NE, HID = 4096, 1024, 16, 64, 8, 4096
TPC = 2048
NT = TPC // 128
NTF = S // 128
NPAIR = 8
SHIFT = -40.0
EPS = 1e-5
R1024 = 1.0 / 1024.0

ENGS = ("t", "v", "s", "y", "g")


class Buf:
    __slots__ = ("w", "r")

    def __init__(self):
        self.w = None
        self.r = []


class Prog:

    def __init__(self):
        self.ops = {e: [] for e in ENGS}
        self.cnt = {}
        self.marks = {}
        self.semnames = []
        for e in ENGS:
            self._addsem(e)

    def _addsem(self, name):
        if name not in self.cnt:
            self.cnt[name] = 0
            self.semnames.append(name)

    def wait(self, eng, tok):
        if tok is None:
            return
        pe, v = tok
        if pe == eng:
            return
        if self.marks.get((eng, pe), 0) >= v:
            return
        self.marks[(eng, pe)] = v
        self.ops[eng].append(("w", pe, v))

    def pre_write(self, eng, *bufs):
        for b in bufs:
            self.wait(eng, b.w)
            for t in b.r:
                self.wait(eng, t)

    def set_write(self, tok, *bufs):
        for b in bufs:
            b.w = tok
            b.r = []

    def step(self, eng, fns, reads=(), writes=(), dma=False, sem=None):
        for b in reads:
            self.wait(eng, b.w)
        self.pre_write(eng, *writes)
        if not isinstance(fns, (list, tuple)):
            fns = [fns]
        sname = sem if sem is not None else eng
        self._addsem(sname)
        if dma:
            for f in fns:
                self.cnt[sname] += 16
                self.ops[eng].append(("o", f, 16, sname))
        else:
            for f in fns[:-1]:
                self.ops[eng].append(("o", f, 0, sname))
            self.cnt[sname] += 1
            self.ops[eng].append(("o", fns[-1], 1, sname))
        tok = (sname, self.cnt[sname])
        for b in reads:
            b.r.append(tok)
        self.set_write(tok, *writes)
        return tok

    def fence(self, eng):
        self.ops[eng].append(("w", eng, self.cnt[eng]))

    def barrier(self):
        for wtr in ENGS:
            for pr in self.semnames:
                if pr != wtr and self.cnt[pr] > self.marks.get((wtr, pr), 0):
                    self.marks[(wtr, pr)] = self.cnt[pr]
                    self.ops[wtr].append(("w", pr, self.cnt[pr]))


def build(debug=0):
    nc = bass.Bass("TRN2", target_bir_lowering=False)

    xd = nc.dram_tensor("x", [S, D], F32, kind="ExternalInput").ap()
    wqd = nc.dram_tensor("wq", [D, D], F32, kind="ExternalInput").ap()
    wkd = nc.dram_tensor("wk", [D, D], F32, kind="ExternalInput").ap()
    wvd = nc.dram_tensor("wv", [D, D], F32, kind="ExternalInput").ap()
    wod = nc.dram_tensor("wo", [D, D], F32, kind="ExternalInput").ap()
    cosd = nc.dram_tensor("cos_t", [128, S], F32, kind="ExternalInput").ap()
    sind = nc.dram_tensor("sin_t", [128, S], F32, kind="ExternalInput").ap()
    gwd = nc.dram_tensor("gate_w", [D, NE], F32, kind="ExternalInput").ap()
    w1d = nc.dram_tensor("w1p", [NE * 8 * 128, 4096], F32, kind="ExternalInput").ap()
    w2d = nc.dram_tensor("w2p", [NE * 8 * 128, 4096], F32, kind="ExternalInput").ap()
    b1d = nc.dram_tensor("b1r", [128, 256], F32, kind="ExternalInput").ap()
    b2d = nc.dram_tensor("b2", [NE, D], F32, kind="ExternalInput").ap()
    gammad = nc.dram_tensor("gamma", [1, D], F32, kind="ExternalInput").ap()
    betad = nc.dram_tensor("beta", [1, D], F32, kind="ExternalInput").ap()
    identbd = nc.dram_tensor("identb", [128, 128], BF16, kind="ExternalInput").ap()
    identfd = nc.dram_tensor("identf", [128, 128], F32, kind="ExternalInput").ap()
    permtd = nc.dram_tensor("permT", [128, 128], BF16, kind="ExternalInput").ap()
    onescd = nc.dram_tensor("ones_c", [128, 1], BF16, kind="ExternalInput").ap()
    onesrfd = nc.dram_tensor("ones_rf", [128, 128], F32, kind="ExternalInput").ap()
    yd = nc.dram_tensor("y", [TPC, D], F32, kind="ExternalOutput").ap()

    dbg = {}
    if debug >= 1:
        dbg["x0T"] = nc.dram_tensor("dbg_x0T", [128, 8 * S], BF16, kind="ExternalOutput").ap()
        dbg["x0Tb"] = nc.dram_tensor("dbg_x0Tb", [128, 8 * S], BF16, kind="ExternalOutput").ap()
        dbg["x0Tc"] = nc.dram_tensor("dbg_x0Tc", [128, 8 * S], BF16, kind="ExternalOutput").ap()
    if debug >= 2:
        dbg["q0"] = nc.dram_tensor("dbg_q0", [128, TPC], BF16, kind="ExternalOutput").ap()
        dbg["k0"] = nc.dram_tensor("dbg_k0", [128, S], BF16, kind="ExternalOutput").ap()
        dbg["v0"] = nc.dram_tensor("dbg_v0", [128, S], BF16, kind="ExternalOutput").ap()
        dbg["attnT"] = nc.dram_tensor("dbg_attnT", [128, 8 * TPC], BF16, kind="ExternalOutput").ap()
    if debug >= 3:
        dbg["x1"] = nc.dram_tensor("dbg_x1", [128, NT * D], F32, kind="ExternalOutput").ap()
        dbg["wot"] = nc.dram_tensor("dbg_wot", [128, 8 * D], BF16, kind="ExternalOutput").ap()
        dbg["z0"] = nc.dram_tensor("dbg_z0", [128, D], F32, kind="ExternalOutput").ap()
        dbg["st0"] = nc.dram_tensor("dbg_st0", [128, 16], F32, kind="ExternalOutput").ap()
    if debug >= 4:
        dbg["xnT"] = nc.dram_tensor("dbg_xnT", [128, 8 * TPC], BF16, kind="ExternalOutput").ap()
        dbg["cw"] = nc.dram_tensor("dbg_cw", [128, 128], F32, kind="ExternalOutput").ap()
    if debug >= 5:
        dbg["hT"] = nc.dram_tensor("dbg_hT", [128, 32 * 512], BF16, kind="ExternalOutput").ap()

    p = Prog()
    B = lambda: Buf()

    st0 = ExitStack()
    sbuf = lambda stk, name, shape, dt, **kw: stk.enter_context(nc.sbuf_tensor(name, shape, dt, **kw))
    psumt = lambda stk, name, shape, dt: stk.enter_context(nc.psum_tensor(name, shape, dt))

    consts = sbuf(st0, "consts", [128, 8], F32)
    identb = sbuf(st0, "identb_s", [128, 128], BF16)
    identf = sbuf(st0, "identf_s", [128, 128], F32)
    permt = sbuf(st0, "permt_s", [128, 128], BF16)
    onesc = sbuf(st0, "onesc_s", [128, 1], BF16)
    onesrf = sbuf(st0, "onesrf_s", [128, 128], F32)
    b1sb = sbuf(st0, "b1sb", [128, 256], F32)
    bnbuf = sbuf(st0, "bnbuf", [128, 12], F32)
    gsb = sbuf(st0, "gsb", [128, 64], F32)
    b2sb = sbuf(st0, "b2sb", [8, D], BF16)
    stats = sbuf(st0, "stats", [128, 16], F32)

    b_consts, b_identb, b_identf, b_permt, b_onesc, b_onesrf = B(), B(), B(), B(), B(), B()
    b_gsm, b_bsm, b_gbc, b_bbc, b_b1, b_gsb, b_b2 = B(), B(), B(), B(), B(), B(), B()
    b_cwtm, b_cwb, b_cwT, b_stats = B(), B(), B(), B()

    psA = ExitStack()
    pf = [psumt(psA, f"pf{i}", [128, 512], F32) for i in range(6)]
    pb = [psumt(psA, f"pb{i}", [128, 1024], BF16) for i in range(2)]
    b_pf = [B() for _ in range(6)]
    b_pb = [B() for _ in range(2)]

    p.step("v", [lambda e: e.memset(consts[:, 0:1], SHIFT),
                 lambda e: e.memset(consts[:, 1:2], EPS)], writes=[b_consts])
    p.step("y", lambda e: e.dma_start(identb[:, :], identbd[:, :]), writes=[b_identb], dma=True, sem="ycst")
    p.step("y", lambda e: e.dma_start(identf[:, :], identfd[:, :]), writes=[b_identf], dma=True, sem="ycst")
    p.step("y", lambda e: e.dma_start(permt[:, :], permtd[:, :]), writes=[b_permt], dma=True, sem="ycst")
    p.step("y", lambda e: e.dma_start(onesc[:, :], onescd[:, :]), writes=[b_onesc], dma=True, sem="ycst")
    p.step("y", lambda e: e.dma_start(onesrf[:, :], onesrfd[:, :]), writes=[b_onesrf], dma=True, sem="ycst")
    p.step("y", lambda e: e.dma_start(b1sb[:, :], b1d[:, :]), writes=[b_b1], dma=True, sem="ycst")
    p.step("y", [(lambda e, dc=dc: e.dma_start(gsb[:, dc * 8:(dc + 1) * 8],
                                               gwd[dc * 128:(dc + 1) * 128, :]))
                 for dc in range(8)], writes=[b_gsb], dma=True, sem="ycst")
    p.step("g", lambda e: e.dma_start(b2sb[:, :], b2d[:, :]), writes=[b_b2], dma=True, sem="gc")

    stA = ExitStack()
    cossb = sbuf(stA, "cossb", [128, S], F32)
    sinsb = sbuf(stA, "sinsb", [128, S], F32)
    x0T = sbuf(stA, "x0T", [128, 8 * S], BF16)
    wst = [sbuf(stA, f"wst{i}", [128, D], BF16) for i in range(2)]
    wqt = sbuf(stA, "wqt", [128, D], BF16)
    wkt = sbuf(stA, "wkt", [128, D], BF16)
    wvt = sbuf(stA, "wvt", [128, D], BF16)
    Q2 = [sbuf(stA, f"Q2{i}", [128, TPC], BF16) for i in range(2)]
    K2 = [sbuf(stA, f"K2{i}", [128, S], BF16) for i in range(2)]
    V2 = [sbuf(stA, f"V2{i}", [128, S], BF16) for i in range(2)]
    praw = [sbuf(stA, f"praw{i}", [128, 512], BF16) for i in range(2)]
    rt1 = sbuf(stA, "rt1", [128, 512], F32)
    rt2 = sbuf(stA, "rt2", [128, 512], F32)
    ptA = [sbuf(stA, f"ptA{i}", [128, 512], BF16) for i in range(2)]
    ptB = [sbuf(stA, f"ptB{i}", [128, 512], BF16) for i in range(2)]
    zs = sbuf(stA, "zs", [33, 512], F32)
    zbs = sbuf(stA, "zbs", [128, 512], F32)

    b_cos, b_sin, b_x0T = B(), B(), B()
    b_x0Tw = [B() for _ in range(8)]
    b_wst = [B() for _ in range(2)]
    b_wqt, b_wkt, b_wvt = B(), B(), B()
    b_Q2 = [B() for _ in range(2)]
    b_K2 = [B() for _ in range(2)]
    b_V2 = [B() for _ in range(2)]
    b_praw = [B() for _ in range(2)]
    b_rt1, b_rt2 = B(), B()
    b_ptA = [B() for _ in range(2)]
    b_ptB = [B() for _ in range(2)]
    b_zs, b_zbs = B(), B()

    stAttnT = ExitStack()
    attnT = sbuf(stAttnT, "attnT", [128, 8 * TPC], BF16, side="right")
    b_attnT = B()

    p.step("y", lambda e: e.dma_start(cossb[:, :], cosd[:, :]), writes=[b_cos], dma=True, sem="ycst")
    p.step("y", lambda e: e.dma_start(sinsb[:, :], sind[:, :]), writes=[b_sin], dma=True, sem="ycst")
    ycst_tok = ("ycst", p.cnt["ycst"])
    for _b in (b_identb, b_identf, b_permt, b_onesc, b_onesrf, b_b1, b_gsb,
               b_cos, b_sin):
        _b.w = ycst_tok

    stA2 = ExitStack()
    x0tm = [sbuf(stA2, f"x0tm{i}", [128, D], BF16) for i in range(2)]
    b_x0tm = [B() for _ in range(2)]
    for i in range(NTF):
        bx = b_x0tm[i % 2]
        p.step("g", (lambda e, i=i, t=x0tm[i % 2]: e.dma_start(
            t[:, :], xd[i * 128:(i + 1) * 128, :])), writes=[bx], dma=True,
            sem=f"ga{i % 2}")
        bp = b_pb[i % 2]
        p.step("t", [(lambda e, dc=dc, i=i, t=x0tm[i % 2]: e.transpose(
            pb[i % 2][:, dc * 128:(dc + 1) * 128], t[:, dc * 128:(dc + 1) * 128],
            identb[:, :])) for dc in range(8)],
            reads=[bx, b_identb], writes=[bp])
        p.step("v", [(lambda e, dc=dc, i=i: e.tensor_copy(
            x0T[:, dc * S + i * 128: dc * S + (i + 1) * 128],
            pb[i % 2][:, dc * 128:(dc + 1) * 128])) for dc in range(8)],
            reads=[bp], writes=[b_x0T, b_x0Tw[i // 4]])

    if debug >= 1:
        p.step("y", lambda e: e.dma_start(dbg["x0T"][:, :], x0T[:, :]),
               reads=[b_x0T], dma=True, sem="ydmp")

    p.barrier()
    stA2.close()
    psA.close()
    psB = ExitStack()
    pst = [psumt(psB, f"pst{i}", [128, 512], F32) for i in range(4)]
    ppv = psumt(psB, "ppv", [128, 512], F32)
    ppz = psumt(psB, "ppz", [128, 512], F32)
    pgp = psumt(psB, "pgp", [128, 512], F32)
    pgm = psumt(psB, "pgm", [128, 512], F32)
    b_pst = [B() for _ in range(4)]
    b_ppv, b_ppz, b_pgp, b_pgm = B(), B(), B(), B()

    thunks = []

    def drain(n):
        for _ in range(min(n, len(thunks))):
            thunks.pop(0)()

    def queue_prep(pair):
        sset = pair % 2
        Qs, Ks, Vs = Q2[sset], K2[sset], V2[sset]
        bQ, bK, bV = b_Q2[sset], b_K2[sset], b_V2[sset]

        def wprep(dram_ap, dst, b_dst, wring, pair=pair):
            def t1(dram_ap=dram_ap, wring=wring):
                p.step("g", (lambda e, d=dram_ap, t=wst[wring]: e.dma_start(
                    t[:, :], d[pair * 128:(pair + 1) * 128, :])),
                    writes=[b_wst[wring]], dma=True, sem=f"gw{wring}")
            def t2(dst=dst, b_dst=b_dst, wring=wring):
                p.step("y", [(lambda e, dc=dc, t=wst[wring], dst=dst:
                              e.dma_start_transpose(
                    dst[:, dc * 128:(dc + 1) * 128], t[:, dc * 128:(dc + 1) * 128]))
                    for dc in range(8)],
                    reads=[b_wst[wring]], writes=[b_dst], dma=True, sem="ywt")
                b_dst.w = ("ywt", p.cnt["ywt"])
            thunks.append(t1)
            thunks.append(t2)

        wprep(wqd, wqt, b_wqt, 0)
        wprep(wkd, wkt, b_wkt, 1)
        wprep(wvd, wvt, b_wvt, 0)

        def projgrp(gg, wt, b_wt, dst, b_dst, scaled):
            def th_b(gg=gg, wt=wt, b_wt=b_wt, scaled=scaled):
                p.step("t", [(lambda e, dc=dc, gg=gg, wt=wt: e.matmul(
                    pgp[:, :], wt[:, dc * 128:(dc + 1) * 128],
                    x0T[:, dc * S + gg * 512: dc * S + (gg + 1) * 512],
                    start=(dc == 0), stop=(dc == 7))) for dc in range(8)],
                    reads=[b_wt, b_x0Tw[gg]], writes=[b_pgp])
                pr, bpr = praw[gg % 2], b_praw[gg % 2]
                sc = 0.125 if scaled else 1.0
                p.step("v", (lambda e, pr=pr, sc=sc: e.tensor_scalar(
                    pr[:, :], pgp[:, :], sc, None, OP.mult)),
                    reads=[b_pgp], writes=[bpr])
                p.fence("v")
                p.step("v", (lambda e, gg=gg, sc=sc: e.scalar_tensor_tensor(
                    rt1[:, :], pgp[:, :], sc, cossb[:, gg * 512:(gg + 1) * 512],
                    OP.mult, OP.mult)),
                    reads=[b_pgp, b_cos], writes=[b_rt1])
            def th_c(gg=gg, dst=dst, b_dst=b_dst, scaled=scaled):
                pr, bpr = praw[gg % 2], b_praw[gg % 2]
                p.step("t", (lambda e, pr=pr: e.matmul(
                    pgm[:, :], permt[:, :], pr[:, :], start=True, stop=True)),
                    reads=[b_permt, bpr], writes=[b_pgm])
                p.step("v", (lambda e, gg=gg: e.tensor_tensor(
                    rt2[:, :], pgm[:, :], sinsb[:, gg * 512:(gg + 1) * 512], OP.mult)),
                    reads=[b_pgm, b_sin], writes=[b_rt2])
                p.fence("v")
                p.step("v", (lambda e, gg=gg, dst=dst: e.tensor_tensor(
                    dst[:, gg * 512:(gg + 1) * 512], rt1[:, :], rt2[:, :], OP.add)),
                    reads=[b_rt1, b_rt2], writes=[b_dst])
            thunks.extend([th_b, th_c])

        def vtile(vt):
            q = vt % 4
            def th(vt=vt, q=q):
                if q == 0:
                    p.pre_write("t", b_pgm)
                tok = p.step("t", [(lambda e, dc=dc, vt=vt, q=q: e.matmul(
                    pgm[:, q * 128:(q + 1) * 128],
                    x0T[:, dc * S + vt * 128: dc * S + (vt + 1) * 128],
                    wvt[:, dc * 128:(dc + 1) * 128],
                    start=(dc == 0), stop=(dc == 7))) for dc in range(8)],
                    reads=[b_x0Tw[vt // 4], b_wvt], writes=[])
                if q == 3:
                    p.set_write(tok, b_pgm)
                    gv = vt // 4
                    p.step("v", (lambda e, gv=gv, Vs=Vs: e.tensor_copy(
                        Vs[:, gv * 512:(gv + 1) * 512], pgm[:, :])),
                        reads=[b_pgm], writes=[bV])
            thunks.append(th)

        for gg in range(TPC // 512):
            projgrp(gg, wqt, b_wqt, Qs, bQ, True)
        for gg in range(S // 512):
            projgrp(gg, wkt, b_wkt, Ks, bK, False)
        for vt in range(NTF):
            vtile(vt)

    queue_prep(0)
    drain(1000)

    for pair in range(NPAIR):
        sset = pair % 2
        Qs, Ks, Vs = Q2[sset], K2[sset], V2[sset]
        bQ, bK, bV = b_Q2[sset], b_K2[sset], b_V2[sset]
        if pair + 1 < NPAIR:
            queue_prep(pair + 1)
        if debug >= 1 and pair == 4:
            p.step("y", lambda e: e.dma_start(dbg["x0Tb"][:, :], x0T[:, :]),
                   reads=[b_x0T], dma=True, sem="ydmp")
        if debug >= 2 and pair == 0:
            p.step("y", lambda e: e.dma_start(dbg["q0"][:, :], Q2[0][:, :]), reads=[bQ], dma=True, sem="ydmp")
            p.step("y", lambda e: e.dma_start(dbg["k0"][:, :], K2[0][:, :]), reads=[bK], dma=True, sem="ydmp")
            p.step("y", lambda e: e.dma_start(dbg["v0"][:, :], V2[0][:, :]), reads=[bV], dma=True, sem="ydmp")

        for g in range(TPC // 512):
            def emit_st(c, g=g, Ks=Ks, Qs=Qs, bK=bK, bQ=bQ):
                i2 = c % 2
                p.step("t", [
                    (lambda e, c=c, g=g, i=i2: e.matmul(
                        pst[i][:, :], Ks[0:64, c * 128:(c + 1) * 128],
                        Qs[0:64, g * 512:(g + 1) * 512], start=True, stop=True)),
                    (lambda e, c=c, g=g, i=2 + i2: e.matmul(
                        pst[i][:, :], Ks[64:128, c * 128:(c + 1) * 128],
                        Qs[64:128, g * 512:(g + 1) * 512], start=True, stop=True))],
                    reads=[bK, bQ], writes=[b_pst[i2], b_pst[2 + i2]])
                p.step("s", (lambda e, c=c, i=i2: e.activation(
                    ptA[c % 2][:, :], pst[i][:, :], AF.Exp, bias=consts[:, 0:1], scale=1.0)),
                    reads=[b_pst[i2], b_consts], writes=[b_ptA[c % 2]])
                p.step("s", (lambda e, c=c, i=2 + i2: e.activation(
                    ptB[c % 2][:, :], pst[i][:, :], AF.Exp, bias=consts[:, 0:1], scale=1.0)),
                    reads=[b_pst[2 + i2], b_consts], writes=[b_ptB[c % 2]])

            def emit_pvz(c, Vs=Vs, bV=bV):
                return p.step("t", [
                    (lambda e, c=c: e.matmul(
                        ppv[0:64, :], Vs[:, c * 128: c * 128 + 64], ptA[c % 2][:, :],
                        start=(c == 0), stop=(c == NTF - 1), tile_position=(0, 0),
                        skip_group_check=True)),
                    (lambda e, c=c: e.matmul(
                        ppv[64:128, :], Vs[:, c * 128 + 64: c * 128 + 128],
                        ptB[c % 2][:, :],
                        start=(c == 0), stop=(c == NTF - 1), tile_position=(0, 64),
                        skip_group_check=True)),
                    (lambda e, c=c: e.matmul(
                        ppz[0:1, :], onesc[:, :], ptA[c % 2][:, :],
                        start=(c == 0), stop=(c == NTF - 1), tile_position=(0, 0),
                        skip_group_check=True)),
                    (lambda e, c=c: e.matmul(
                        ppz[32:33, :], onesc[:, :], ptB[c % 2][:, :],
                        start=(c == 0), stop=(c == NTF - 1), tile_position=(0, 32),
                        skip_group_check=True))],
                    reads=[bV, b_ptA[c % 2], b_ptB[c % 2], b_onesc], writes=[])

            for c in range(NTF):
                emit_st(c)
                drain(1)
                if c == 1:
                    p.pre_write("t", b_ppv, b_ppz)
                if c >= 1:
                    tok = emit_pvz(c - 1)
            tok = emit_pvz(NTF - 1)
            p.set_write(tok, b_ppv, b_ppz)
            p.step("v", [lambda e: e.tensor_copy(zs[0:1, :], ppz[0:1, :]),
                         lambda e: e.tensor_copy(zs[32:33, :], ppz[32:33, :])],
                   reads=[b_ppz], writes=[b_zs])
            p.step("s", (lambda e: e.activation(rt2[0:33, :], zs[0:33, :], AF.Ln)),
                   reads=[b_zs], writes=[b_rt2])
            p.fence("s")
            p.step("s", (lambda e: e.activation(zs[0:33, :], rt2[0:33, :], AF.Exp, scale=-1.0)),
                   reads=[b_rt2], writes=[b_zs])
            p.step("t", [
                lambda e: e.matmul(pgp[0:64, :], onesrf[0:1, 0:64], zs[0:1, :],
                                   start=True, stop=True, tile_position=(0, 0)),
                lambda e: e.matmul(pgp[64:128, :], onesrf[32:33, 0:64], zs[32:33, :],
                                   start=True, stop=True, tile_position=(32, 64))],
                reads=[b_onesrf, b_zs], writes=[b_pgp])
            p.step("v", lambda e: e.tensor_copy(zbs[:, :], pgp[:, :]),
                   reads=[b_pgp], writes=[b_zbs])
            p.fence("v")
            p.step("v", [
                (lambda e, pair=pair, g=g: e.tensor_tensor(
                    attnT[0:64, pair * TPC + g * 512: pair * TPC + (g + 1) * 512],
                    ppv[0:64, :], zbs[0:64, :], OP.mult)),
                (lambda e, pair=pair, g=g: e.tensor_tensor(
                    attnT[64:128, pair * TPC + g * 512: pair * TPC + (g + 1) * 512],
                    ppv[64:128, :], zbs[64:128, :], OP.mult))],
                reads=[b_ppv, b_zbs], writes=[b_attnT])
        drain(1000)

    if debug >= 1:
        p.step("y", lambda e: e.dma_start(dbg["x0Tc"][:, :], x0T[:, :]),
               reads=[b_x0T], dma=True, sem="ydmp")
    if debug >= 2:
        p.step("y", lambda e: e.dma_start(dbg["attnT"][:, :], attnT[:, :]),
               reads=[b_attnT], dma=True, sem="ydmp")

    p.barrier()
    psB.close()
    psC = ExitStack()
    pf = [psumt(psC, f"pfc{i}", [128, 512], F32) for i in range(6)]
    pb = [psumt(psC, f"pbc{i}", [128, 1024], BF16) for i in range(2)]
    b_pf = [B() for _ in range(6)]
    b_pb = [B() for _ in range(2)]

    p.barrier()
    stA.close()
    stX = ExitStack()
    x1 = sbuf(stX, "x1", [128, NT * D], BF16)
    b_x1 = B()
    stC = ExitStack()
    wot = sbuf(stC, "wot", [128, 8 * D], BF16)
    wstC = [sbuf(stC, f"wstC{i}", [128, D], BF16) for i in range(2)]
    x0re = [sbuf(stC, f"x0re{i}", [128, D], F32) for i in range(2)]
    zC = [sbuf(stC, f"zC{i}", [128, D], F32) for i in range(2)]
    b_wot = B()
    b_wstC = [B() for _ in range(2)]
    b_x0re = [B() for _ in range(2)]
    b_zC = [B() for _ in range(2)]

    for nchunk in range(8):
        bw = b_wstC[nchunk % 2]
        p.step("g", (lambda e, n=nchunk, t=wstC[nchunk % 2]: e.dma_start(
            t[:, :], wod[n * 128:(n + 1) * 128, :])), writes=[bw], dma=True,
            sem=f"gwc{nchunk % 2}")
        bp = b_pb[nchunk % 2]
        p.step("t", [(lambda e, dc=dc, n=nchunk, t=wstC[nchunk % 2]: e.transpose(
            pb[n % 2][:, dc * 128:(dc + 1) * 128], t[:, dc * 128:(dc + 1) * 128],
            identb[:, :])) for dc in range(8)], reads=[bw, b_identb], writes=[bp])
        p.step("v", [(lambda e, dc=dc, n=nchunk: e.tensor_copy(
            wot[:, dc * D + n * 128: dc * D + (n + 1) * 128],
            pb[n % 2][:, dc * 128:(dc + 1) * 128])) for dc in range(8)],
            reads=[bp], writes=[b_wot])

    def ln_stats(ten, c0, b_z, rvar_col):
        del rvar_col
        p.fence("v")
        p.step("v", [
            (lambda e: e.bn_stats(bnbuf[:, 0:6], ten[:, c0:c0 + 512])),
            (lambda e: e.bn_stats(bnbuf[:, 6:12], ten[:, c0 + 512:c0 + 1024]))],
            reads=[b_z], writes=[b_stats])
        p.fence("v")
        p.step("v", (lambda e: e.bn_aggr(stats[:, 2:4], bnbuf[:, 0:12])),
               reads=[b_z], writes=[b_stats])
        p.step("s", (lambda e: e.activation(stats[:, 5:6], stats[:, 3:4], AF.Ln,
                                            bias=consts[:, 1:2])),
               reads=[b_stats, b_consts], writes=[b_stats])
        p.fence("s")
        p.step("s", (lambda e: e.activation(stats[:, 10:11], stats[:, 5:6], AF.Exp,
                                            scale=-0.5)),
               reads=[b_stats], writes=[b_stats])
        p.step("v", (lambda e: e.tensor_tensor(stats[:, 4:5], stats[:, 2:3],
                                               stats[:, 10:11], OP.mult)),
               reads=[b_stats], writes=[b_stats])
        p.fence("v")
        p.step("v", (lambda e: e.tensor_scalar(stats[:, 11:12], stats[:, 4:5],
                                               -1.0, None, OP.mult)),
               reads=[b_stats], writes=[b_stats])
        p.fence("v")

    for tt in range(NT):
        bx0 = b_x0re[tt % 2]
        p.step("y", (lambda e, tt=tt, t=x0re[tt % 2]: e.dma_start(
            t[:, :], xd[tt * 128:(tt + 1) * 128, :])), writes=[bx0], dma=True,
            sem=f"yre{tt % 2}")
        bL, bH = b_pf[(tt % 2) * 2], b_pf[(tt % 2) * 2 + 1]
        mms = []
        for dc in range(8):
            mms.append(lambda e, dc=dc, tt=tt, i=(tt % 2) * 2: e.matmul(
                pf[i][:, :], attnT[:, dc * TPC + tt * 128: dc * TPC + (tt + 1) * 128],
                wot[:, dc * D: dc * D + 512], start=(dc == 0), stop=(dc == 7)))
            mms.append(lambda e, dc=dc, tt=tt, i=(tt % 2) * 2 + 1: e.matmul(
                pf[i][:, :], attnT[:, dc * TPC + tt * 128: dc * TPC + (tt + 1) * 128],
                wot[:, dc * D + 512: dc * D + 1024], start=(dc == 0), stop=(dc == 7)))
        p.step("t", mms, reads=[b_attnT, b_wot], writes=[bL, bH])
        bz = b_zC[tt % 2]
        p.step("v", [
            (lambda e, tt=tt, i=(tt % 2) * 2, z=zC[tt % 2]: e.tensor_tensor(
                z[:, 0:512], pf[i][:, :], x0re[tt % 2][:, 0:512], OP.add)),
            (lambda e, tt=tt, i=(tt % 2) * 2 + 1, z=zC[tt % 2]: e.tensor_tensor(
                z[:, 512:1024], pf[i][:, :], x0re[tt % 2][:, 512:1024], OP.add))],
            reads=[bL, bH, bx0], writes=[bz])
        ln_stats(zC[tt % 2], 0, bz, 8)
        if debug >= 3 and tt == 0:
            p.step("y", lambda e: e.dma_start(dbg["z0"][:, :], zC[0][:, :]),
                   reads=[bz], dma=True, sem="ydmp")
            p.step("y", lambda e: e.dma_start(dbg["st0"][:, :], stats[:, :]),
                   reads=[b_stats], dma=True, sem="ydmp")
        p.step("v", (lambda e, tt=tt, z=zC[tt % 2]: e.tensor_scalar(
            x1[:, tt * D:(tt + 1) * D], z[:, :],
            stats[:, 10:11], stats[:, 11:12], OP.mult, OP.add)),
            reads=[bz, b_stats], writes=[b_x1])

    if debug >= 3:
        p.step("y", lambda e: e.dma_start(dbg["x1"][:, :], x1[:, :]), reads=[b_x1], dma=True, sem="ydmp")
        p.step("y", lambda e: e.dma_start(dbg["wot"][:, :], wot[:, :]), reads=[b_wot], dma=True, sem="ydmp")

    p.barrier()
    stC.close()
    stAttnT.close()
    stD = ExitStack()
    xnT = sbuf(stD, "xnT", [128, 8 * TPC], BF16)
    cw_tm = sbuf(stD, "cw_tm", [128, 128], F32)
    cwb = sbuf(stD, "cwb", [128, 128], BF16)
    cwT = sbuf(stD, "cwT", [8, TPC], BF16)
    b_xnT = B()
    stD2 = ExitStack()
    gsm = sbuf(stD2, "gsm", [1, D], F32)
    bsm = sbuf(stD2, "bsm", [1, D], F32)
    gbc = sbuf(stD2, "gbc", [128, D], F32)
    bbc = sbuf(stD2, "bbc", [128, D], F32)
    xcD = [sbuf(stD2, f"xcD{i}", [128, D], F32) for i in range(2)]
    xnD = [sbuf(stD2, f"xnD{i}", [128, D], F32) for i in range(2)]
    xnT32 = [sbuf(stD2, f"xnT32{i}", [128, D], F32) for i in range(2)]
    lgD = sbuf(stD2, "lgD", [128, 48], F32)
    b_xcD = [B() for _ in range(2)]
    b_xnD = [B() for _ in range(2)]
    b_xnT32 = [B() for _ in range(2)]
    b_lgD = B()

    p.step("y", lambda e: e.dma_start(gsm[:, :], gammad[:, :]), writes=[b_gsm], dma=True, sem="ygb")
    p.step("y", lambda e: e.dma_start(bsm[:, :], betad[:, :]), writes=[b_bsm], dma=True, sem="ygb")
    ygb_tok = ("ygb", p.cnt["ygb"])
    b_gsm.w = ygb_tok
    b_bsm.w = ygb_tok
    for src, bsrc, dst, bdst in ((gsm, b_gsm, gbc, b_gbc), (bsm, b_bsm, bbc, b_bbc)):
        p.step("t", [(lambda e, h=h, src=src: e.matmul(
            pf[h][:, :], onesrf[0:1, 0:128], src[0:1, h * 512:(h + 1) * 512],
            start=True, stop=True)) for h in range(2)],
            reads=[b_onesrf, bsrc], writes=[b_pf[0], b_pf[1]])
        p.step("v", [(lambda e, h=h, dst=dst: e.tensor_copy(
            dst[:, h * 512:(h + 1) * 512], pf[h][:, :])) for h in range(2)],
            reads=[b_pf[0], b_pf[1]], writes=[bdst])

    for tt in range(NT):
        ln_stats(x1, tt * D, b_x1, 8)
        bxc = b_xcD[tt % 2]
        p.step("v", (lambda e, tt=tt, t=xcD[tt % 2]: e.tensor_scalar(
            t[:, :], x1[:, tt * D:(tt + 1) * D],
            stats[:, 10:11], stats[:, 11:12], OP.mult, OP.add)),
            reads=[b_x1, b_stats], writes=[bxc])
        bxn = b_xnD[tt % 2]
        p.fence("v")
        p.step("v", (lambda e, t=xcD[tt % 2], o=xnD[tt % 2]: e.tensor_tensor(
                o[:, :], t[:, :], gbc[:, :], OP.mult)),
            reads=[bxc, b_gbc], writes=[bxn])
        p.fence("v")
        p.step("v", (lambda e, o=xnD[tt % 2]: e.tensor_tensor(
                o[:, :], o[:, :], bbc[:, :], OP.add)),
            reads=[b_bbc], writes=[bxn])
        bq0, bq1 = b_pf[tt % 2], b_pf[2 + tt % 2]
        p.step("t", [(lambda e, j=j, tt=tt, t=xnD[tt % 2], i=tt % 2: e.transpose(
            pf[i][:, j * 128:(j + 1) * 128], t[:, j * 128:(j + 1) * 128], identf[:, :]))
            for j in range(4)], reads=[bxn, b_identf], writes=[bq0])
        p.step("t", [(lambda e, j=j, tt=tt, t=xnD[tt % 2], i=2 + tt % 2: e.transpose(
            pf[i][:, (j - 4) * 128:(j - 3) * 128], t[:, j * 128:(j + 1) * 128],
            identf[:, :])) for j in range(4, 8)], reads=[bxn, b_identf], writes=[bq1])
        evs = []
        for dc in range(8):
            i = (tt % 2) if dc < 4 else (2 + tt % 2)
            off = (dc % 4) * 128
            evs.append(lambda e, dc=dc, tt=tt, i=i, off=off: e.tensor_copy(
                xnT[:, dc * TPC + tt * 128: dc * TPC + (tt + 1) * 128],
                pf[i][:, off:off + 128]))
        p.step("v", evs, reads=[bq0, bq1], writes=[b_xnT])
        bx32 = b_xnT32[tt % 2]
        p.step("v", [
            (lambda e, tt=tt, i=tt % 2, t=xnT32[tt % 2]: e.tensor_copy(
                t[:, 0:512], pf[i][:, :])),
            (lambda e, tt=tt, i=2 + tt % 2, t=xnT32[tt % 2]: e.tensor_copy(
                t[:, 512:1024], pf[i][:, :]))],
            reads=[bq0, bq1], writes=[bx32])
        bg = b_pf[4]
        p.step("t", [(lambda e, dc=dc, tt=tt, t=xnT32[tt % 2]: e.matmul(
            pf[4][:, 0:8], t[:, dc * 128:(dc + 1) * 128], gsb[:, dc * 8:(dc + 1) * 8],
            start=(dc == 0), stop=(dc == 7))) for dc in range(8)],
            reads=[bx32, b_gsb], writes=[bg])
        c0, c1, c2, c3, c4 = (lgD[:, 0:8], lgD[:, 8:16], lgD[:, 16:24],
                              lgD[:, 24:32], lgD[:, 32:40])
        m1, m2, ssum = lgD[:, 40:41], lgD[:, 41:42], lgD[:, 42:43]
        p.step("v", (lambda e: e.reduce_max(m1, pf[4][:, 0:8], AXX)),
            reads=[bg], writes=[b_lgD])
        p.fence("v")
        p.step("v", (lambda e: e.tensor_scalar(c0, pf[4][:, 0:8], m1, None, OP.subtract)),
            reads=[bg], writes=[b_lgD])
        p.step("s", (lambda e: e.activation(c1, c0, AF.Exp)), reads=[b_lgD], writes=[b_lgD])
        p.fence("v")
        p.step("v", (lambda e: e.tensor_scalar(c2, c0, 0.0, None, OP.is_ge)),
            reads=[b_lgD], writes=[b_lgD])
        p.fence("v")
        p.step("v", (lambda e: e.tensor_scalar(c2, c2, -1e30, None, OP.mult)),
            reads=[b_lgD], writes=[b_lgD])
        p.fence("v")
        p.step("v", (lambda e: e.tensor_tensor(c2, c0, c2, OP.add)),
            reads=[b_lgD], writes=[b_lgD])
        p.fence("v")
        p.step("v", (lambda e: e.reduce_max(m2, c2, AXX)),
            reads=[b_lgD], writes=[b_lgD])
        p.fence("v")
        p.step("v", (lambda e: e.tensor_scalar(c3, c0, m2, None, OP.is_ge)),
            reads=[b_lgD], writes=[b_lgD])
        p.fence("v")
        p.step("v", (lambda e: e.tensor_tensor(c4, c1, c3, OP.mult)),
            reads=[b_lgD], writes=[b_lgD])
        p.fence("v")
        p.step("v", (lambda e: e.tensor_reduce(ssum, c4, AXX, OP.add)),
            reads=[b_lgD], writes=[b_lgD])
        p.step("s", (lambda e: e.activation(lgD[:, 43:44], ssum, AF.Ln)),
            reads=[b_lgD], writes=[b_lgD])
        p.fence("s")
        p.step("s", (lambda e: e.activation(m2, lgD[:, 43:44], AF.Exp, scale=-1.0)),
            reads=[b_lgD], writes=[b_lgD])
        p.fence("v")
        p.step("v", (lambda e, tt=tt: e.tensor_scalar(
            cw_tm[:, tt * 8:(tt + 1) * 8], c4, m2, None, OP.mult)),
            reads=[b_lgD], writes=[b_cwtm])

    p.step("v", lambda e: e.tensor_copy(cwb[:, :], cw_tm[:, :]),
           reads=[b_cwtm], writes=[b_cwb])
    for half in range(2):
        bp = b_pb[half]
        p.step("t", [(lambda e, k=k, half=half: e.transpose(
            pb[half][0:8, k * 128:(k + 1) * 128],
            cwb[:, (half * 8 + k) * 8:(half * 8 + k + 1) * 8], identb[:, :]))
            for k in range(8)], reads=[b_cwb, b_identb], writes=[bp])
        p.step("v", [(lambda e, k=k, half=half: e.tensor_copy(
            cwT[0:8, (half * 8 + k) * 128:(half * 8 + k + 1) * 128],
            pb[half][0:8, k * 128:(k + 1) * 128])) for k in range(8)],
            reads=[bp], writes=[b_cwT])

    if debug >= 4:
        p.step("y", lambda e: e.dma_start(dbg["xnT"][:, :], xnT[:, :]), reads=[b_xnT], dma=True, sem="ydmp")
        p.step("y", lambda e: e.dma_start(dbg["cw"][:, :], cw_tm[:, :]), reads=[b_cwtm], dma=True, sem="ydmp")

    p.barrier()
    stD2.close()
    psC.close()
    p.barrier()
    psE = ExitStack()
    po = [psumt(psE, f"po{i}", [128, 512], F32) for i in range(8)]
    b_po = [B() for _ in range(8)]
    stE = ExitStack()
    hT = sbuf(stE, "hT", [128, 32 * 512], BF16)
    acc = sbuf(stE, "acc", [128, 4 * D], F32)
    w1b = [sbuf(stE, f"w1b{i}", [128, 4096], BF16) for i in range(3)]
    w2b = [sbuf(stE, f"w2b{i}", [128, 4096], BF16) for i in range(3)]
    tmpE = [sbuf(stE, f"tmpE{i}", [128, 512], F32) for i in range(2)]
    zE = [sbuf(stE, "zE0", [128, D], F32)] * 2
    outE = [sbuf(stE, f"outE{i}", [128, D], F32) for i in range(2)]
    b_hT, b_acc = B(), B()
    b_w1b = [B() for _ in range(3)]
    b_w2b = [B() for _ in range(3)]
    b_tmpE = [B() for _ in range(2)]
    b_zE = [B()] * 2
    b_outE = [B() for _ in range(2)]

    def emit_b2_init(tg):
        for tt in range(4):
            ttg = tg * 4 + tt
            bb0, bb1 = b_po[6], b_po[7]
            p.step("t", [
                (lambda e, ttg=ttg: e.matmul(
                    po[6][:, :], cwT[0:8, ttg * 128:(ttg + 1) * 128],
                    b2sb[0:8, 0:512], start=True, stop=True)),
                (lambda e, ttg=ttg: e.matmul(
                    po[7][:, :], cwT[0:8, ttg * 128:(ttg + 1) * 128],
                    b2sb[0:8, 512:1024], start=True, stop=True))],
                reads=[b_cwT, b_b2], writes=[bb0, bb1])
            p.step("v", [
                (lambda e, tt=tt: e.tensor_copy(acc[:, tt * D: tt * D + 512], po[6][:, :])),
                (lambda e, tt=tt: e.tensor_copy(acc[:, tt * D + 512: tt * D + 1024],
                                                po[7][:, :]))],
                reads=[bb0, bb1], writes=[b_acc])

    def emit_final(tg):
        for tt in range(4):
            ttg = tg * 4 + tt
            bz = b_zE[tt % 2]
            p.fence("v")
            p.step("v", (lambda e, ttg=ttg, t=zE[tt % 2]: e.tensor_scalar(
                t[:, :], x1[:, ttg * D:(ttg + 1) * D], 2.0, None, OP.mult)),
                reads=[b_x1], writes=[bz])
            p.fence("v")
            p.step("v", (lambda e, tt=tt, t=zE[tt % 2]: e.tensor_tensor(
                t[:, :], t[:, :], acc[:, tt * D:(tt + 1) * D], OP.add)),
                reads=[b_acc], writes=[bz])
            ln_stats(zE[tt % 2], 0, bz, 8)
            bo = b_outE[tt % 2]
            p.step("v", (lambda e, tt=tt, z=zE[tt % 2], o=outE[tt % 2]: e.tensor_scalar(
                o[:, :], z[:, :], stats[:, 10:11], stats[:, 11:12], OP.mult, OP.add)),
                reads=[bz, b_stats], writes=[bo])
            p.step("y", (lambda e, ttg=ttg, o=outE[tt % 2]: e.dma_start(
                yd[ttg * 128:(ttg + 1) * 128, :], o[:, :])), reads=[bo], dma=True,
                sem=f"yo{tt % 2}")


    for tg in range(4):
        for ex in range(NE):
            def dma_w1(js, ex=ex):
                p.step("g", (lambda e, ex=ex, js=js, t=w1b[js % 3]: e.dma_start(
                    t[:, :], w1d[(ex * 8 + js) * 128:(ex * 8 + js + 1) * 128, :])),
                    writes=[b_w1b[js % 3]], dma=True, sem=f"sw1_{js % 3}")

            def dma_w2(jg, ex=ex):
                p.step("g", (lambda e, ex=ex, jg=jg, t=w2b[jg % 3]: e.dma_start(
                    t[:, :], w2d[(ex * 8 + jg) * 128:(ex * 8 + jg + 1) * 128, :])),
                    writes=[b_w2b[jg % 3]], dma=True, sem=f"sw2_{jg % 3}")

            if ex == 0 and tg == 0:
                dma_w1(0)
                dma_w1(1)
                dma_w1(2)
            for js in range(8):
                for jj in range(4):
                    jc = js * 4 + jj
                    bh = b_po[jc % 2]
                    p.step("t", [(lambda e, dc=dc, js=js, jj=jj, tg=tg, i=jc % 2,
                                  t=w1b[js % 3]: e.matmul(
                        po[i][:, :], t[:, dc * 512 + jj * 128: dc * 512 + (jj + 1) * 128],
                        xnT[:, dc * TPC + tg * 512: dc * TPC + (tg + 1) * 512],
                        start=(dc == 0), stop=(dc == 7))) for dc in range(8)],
                        reads=[b_w1b[js % 3], b_xnT], writes=[bh])
                    p.step("s", (lambda e, jc=jc, ex=ex, i=jc % 2: e.activation(
                        hT[:, jc * 512:(jc + 1) * 512], po[i][:, :], AF.Gelu,
                        bias=b1sb[:, ex * 32 + jc: ex * 32 + jc + 1], scale=1.0)),
                        reads=[bh, b_b1], writes=[b_hT])
                if js + 3 < 8:
                    dma_w1(js + 3)
                elif js == 5:
                    dma_w2(0)
                elif js == 6:
                    dma_w2(1)
                elif js == 7:
                    dma_w2(2)
            nex, ntg = (ex + 1, tg) if ex + 1 < NE else (0, tg + 1)
            if ntg < 4:
                for sl in range(3):
                    p.step("g", (lambda e, nex=nex, sl=sl: e.dma_start(
                        w1b[sl][:, :],
                        w1d[(nex * 8 + sl) * 128:(nex * 8 + sl + 1) * 128, :])),
                        writes=[b_w1b[sl]], dma=True, sem=f"sw1_{sl}")
            if ex == 0:
                emit_b2_init(tg)
            for k in range(8):
                p.pre_write("t", b_po[k])
            tok = None
            for jg in range(8):
                for ci in range(4):
                    jc = jg * 4 + ci
                    mms = []
                    for tt in range(4):
                        for hh in range(2):
                            mms.append(lambda e, jc=jc, tt=tt, hh=hh, ci=ci,
                                       t=w2b[jg % 3]: e.matmul(
                                po[tt * 2 + hh][:, :],
                                hT[:, jc * 512 + tt * 128: jc * 512 + (tt + 1) * 128],
                                t[:, ci * 1024 + hh * 512: ci * 1024 + (hh + 1) * 512],
                                start=(jc == 0), stop=(jc == 31), skip_group_check=True))
                    tok = p.step("t", mms, reads=[b_hT, b_w2b[jg % 3]], writes=[])
                if jg + 3 < 8:
                    dma_w2(jg + 3)
            p.set_write(tok, *b_po)
            if debug >= 5 and tg == 0 and ex == 0:
                p.step("y", lambda e: e.dma_start(dbg["hT"][:, :], hT[:, :]),
                       reads=[b_hT], dma=True, sem="ydmp")
            for tt in range(4):
                ttg = tg * 4 + tt
                for hh in range(2):
                    k = tt * 2 + hh
                    btm = b_tmpE[k % 2]
                    p.step("s", (lambda e, k=k, ttg=ttg, ex=ex, t=tmpE[k % 2]: e.activation(
                        t[:, :], po[k][:, :], AF.Copy,
                        scale=cw_tm[:, ttg * 8 + ex: ttg * 8 + ex + 1])),
                        reads=[b_po[k], b_cwtm], writes=[btm])
                    p.step("v", (lambda e, tt=tt, hh=hh, k=k, t=tmpE[k % 2]: e.tensor_tensor(
                        acc[:, tt * D + hh * 512: tt * D + (hh + 1) * 512],
                        acc[:, tt * D + hh * 512: tt * D + (hh + 1) * 512],
                        t[:, :], OP.add)), reads=[btm, b_acc], writes=[b_acc])

        emit_final(tg)

    p.ops["y"].append(("w", "yo0", p.cnt["yo0"]))
    p.ops["y"].append(("w", "yo1", p.cnt["yo1"]))

    with ExitStack() as es:
        sems = {e: es.enter_context(nc.semaphore(f"sem_{e}")) for e in p.semnames}
        block = es.enter_context(nc.Block())

        def runner(ename):
            def _run(eng):
                for item in p.ops[ename]:
                    if item[0] == "w":
                        eng.wait_ge(sems[item[1]], item[2])
                    else:
                        inst = item[1](eng)
                        if item[2]:
                            inst.then_inc(sems[item[3]], item[2])
            return _run

        block.tensor(runner("t"))
        block.vector(runner("v"))
        block.scalar(runner("s"))
        block.sync(runner("y"))
        block.gpsimd(runner("g"))

    psE.close()
    stE.close()
    stD.close()
    stX.close()
    st0.close()
    return nc



def _tables():
    inv_freq = 1.0 / (10000.0 ** (np.arange(0, DK, 2, dtype=np.float32) / DK))
    t = np.arange(S, dtype=np.float32)
    freq_row = inv_freq[(np.arange(128) % 64) % 32]
    ang = freq_row[:, None] * t[None, :]
    return np.cos(ang).astype(np.float32), np.sin(ang).astype(np.float32)


def _perm_matrix():
    P = np.zeros((64, 64), dtype=np.float32)
    for j in range(32):
        P[j, 2 * j + 1] = -1.0
    for i in range(32):
        P[32 + i, 2 * i] = 1.0
    blk = np.zeros((128, 128), dtype=np.float32)
    blk[0:64, 0:64] = P.T
    blk[64:128, 64:128] = P.T
    return blk.astype(BF)


_CACHE = {}


def _get_nc(debug=0):
    key = ("nc", debug)
    if key not in _CACHE:
        _CACHE[key] = build(debug)
    return _CACHE[key]


def make_in_maps(L0_attn_x, L0_attn_Wq, L0_attn_Wk, L0_attn_Wv, L0_attn_Wo,
                 L0_ffn_gate_w, L0_ffn_w1, L0_ffn_b1, L0_ffn_w2, L0_ffn_b2,
                 L0_ffn_gamma, L0_ffn_beta):
    x = np.ascontiguousarray(np.asarray(L0_attn_x, dtype=np.float32))
    wq = np.ascontiguousarray(np.asarray(L0_attn_Wq, dtype=np.float32))
    wk = np.ascontiguousarray(np.asarray(L0_attn_Wk, dtype=np.float32))
    wv = np.ascontiguousarray(np.asarray(L0_attn_Wv, dtype=np.float32))
    wo = np.ascontiguousarray(np.asarray(L0_attn_Wo, dtype=np.float32))
    gw = np.ascontiguousarray(np.asarray(L0_ffn_gate_w, dtype=np.float32))
    w1 = np.asarray(L0_ffn_w1, dtype=np.float32)
    b1 = np.asarray(L0_ffn_b1, dtype=np.float32)
    w2 = np.asarray(L0_ffn_w2, dtype=np.float32)
    b2 = np.ascontiguousarray(np.asarray(L0_ffn_b2, dtype=np.float32))
    gamma = np.asarray(L0_ffn_gamma, dtype=np.float32).reshape(1, D)
    beta = np.asarray(L0_ffn_beta, dtype=np.float32).reshape(1, D)

    w1p = np.ascontiguousarray(
        w1.reshape(8, 8, 128, 8, 512).transpose(0, 3, 2, 1, 4).reshape(8192, 4096))
    w2p = np.ascontiguousarray(
        w2.reshape(8, 8, 4, 128, 1024).transpose(0, 1, 3, 2, 4).reshape(8192, 4096))
    b1r = np.ascontiguousarray(b1.reshape(8, 32, 128).transpose(2, 0, 1).reshape(128, 256))
    cos_t, sin_t = _tables()
    identb = np.eye(128, dtype=BF)
    identf = np.eye(128, dtype=np.float32)
    permT = _perm_matrix()
    ones_c = np.ones((128, 1), dtype=BF)
    ones_rf = np.ones((128, 128), dtype=np.float32)

    common = dict(wq=wq, wk=wk, wv=wv, wo=wo, gate_w=gw, w1p=w1p, w2p=w2p,
                  b1r=b1r, b2=b2, gamma=gamma, beta=beta, identb=identb,
                  identf=identf, permT=permT, ones_c=ones_c, ones_rf=ones_rf)
    in_maps = []
    for c in range(8):
        b, half = c // 2, c % 2
        own = slice(half * TPC, (half + 1) * TPC)
        oth = slice((1 - half) * TPC, (2 - half) * TPC)
        xp = np.ascontiguousarray(np.concatenate([x[b, own], x[b, oth]], axis=0))
        cs = np.ascontiguousarray(np.concatenate([cos_t[:, own], cos_t[:, oth]], axis=1))
        sn = np.ascontiguousarray(np.concatenate([sin_t[:, own], sin_t[:, oth]], axis=1))
        m = dict(common)
        m.update(x=xp, cos_t=cs, sin_t=sn)
        in_maps.append(m)
    return in_maps


def kernel(L0_attn_x, L0_attn_Wq, L0_attn_Wk, L0_attn_Wv, L0_attn_Wo,
           L0_ffn_gate_w, L0_ffn_w1, L0_ffn_b1, L0_ffn_w2, L0_ffn_b2,
           L0_ffn_gamma, L0_ffn_beta):
    nc = _get_nc(0)
    in_maps = make_in_maps(L0_attn_x, L0_attn_Wq, L0_attn_Wk, L0_attn_Wv, L0_attn_Wo,
                           L0_ffn_gate_w, L0_ffn_w1, L0_ffn_b1, L0_ffn_w2, L0_ffn_b2,
                           L0_ffn_gamma, L0_ffn_beta)
    res = run_bass_kernel_spmd(nc, in_maps, core_ids=list(range(8)))
    out = np.empty((4, S, D), dtype=np.float32)
    for c in range(8):
        b, half = c // 2, c % 2
        out[b, half * TPC:(half + 1) * TPC, :] = np.asarray(
            res.results[c]["y"], dtype=np.float32)
    return out


# version 14
# speedup vs baseline: 1.0487x; 1.0487x over previous
import numpy as np
import ml_dtypes
from contextlib import ExitStack

import concourse.bass as bass
import concourse.mybir as mybir
from concourse.bass_utils import run_bass_kernel_spmd

F32 = mybir.dt.float32
BF16 = mybir.dt.bfloat16
AXX = mybir.AxisListType.X
OP = mybir.AluOpType
AF = mybir.ActivationFunctionType
BF = ml_dtypes.bfloat16

S, D, NH, DK, NE, HID = 4096, 1024, 16, 64, 8, 4096
TPC = 2048
NT = TPC // 128
NTF = S // 128
NPAIR = 8
SHIFT = -40.0
EPS = 1e-5
R1024 = 1.0 / 1024.0

ENGS = ("t", "v", "s", "y", "g")


class Buf:
    __slots__ = ("w", "r")

    def __init__(self):
        self.w = None
        self.r = []


class Prog:

    def __init__(self):
        self.ops = {e: [] for e in ENGS}
        self.cnt = {}
        self.marks = {}
        self.semnames = []
        for e in ENGS:
            self._addsem(e)

    def _addsem(self, name):
        if name not in self.cnt:
            self.cnt[name] = 0
            self.semnames.append(name)

    def wait(self, eng, tok):
        if tok is None:
            return
        pe, v = tok
        if pe == eng:
            return
        if self.marks.get((eng, pe), 0) >= v:
            return
        self.marks[(eng, pe)] = v
        self.ops[eng].append(("w", pe, v))

    def pre_write(self, eng, *bufs):
        for b in bufs:
            self.wait(eng, b.w)
            for t in b.r:
                self.wait(eng, t)

    def set_write(self, tok, *bufs):
        for b in bufs:
            b.w = tok
            b.r = []

    def step(self, eng, fns, reads=(), writes=(), dma=False, sem=None):
        for b in reads:
            self.wait(eng, b.w)
        self.pre_write(eng, *writes)
        if not isinstance(fns, (list, tuple)):
            fns = [fns]
        sname = sem if sem is not None else eng
        self._addsem(sname)
        if dma:
            for f in fns:
                self.cnt[sname] += 16
                self.ops[eng].append(("o", f, 16, sname))
        else:
            for f in fns[:-1]:
                self.ops[eng].append(("o", f, 0, sname))
            self.cnt[sname] += 1
            self.ops[eng].append(("o", fns[-1], 1, sname))
        tok = (sname, self.cnt[sname])
        for b in reads:
            b.r.append(tok)
        self.set_write(tok, *writes)
        return tok

    def fence(self, eng):
        self.ops[eng].append(("w", eng, self.cnt[eng]))

    def barrier(self):
        for wtr in ENGS:
            for pr in self.semnames:
                if pr != wtr and self.cnt[pr] > self.marks.get((wtr, pr), 0):
                    self.marks[(wtr, pr)] = self.cnt[pr]
                    self.ops[wtr].append(("w", pr, self.cnt[pr]))


def build(debug=0):
    nc = bass.Bass("TRN2", target_bir_lowering=False)

    xd = nc.dram_tensor("x", [S, D], F32, kind="ExternalInput").ap()
    wqd = nc.dram_tensor("wq", [D, D], F32, kind="ExternalInput").ap()
    wkd = nc.dram_tensor("wk", [D, D], F32, kind="ExternalInput").ap()
    wvd = nc.dram_tensor("wv", [D, D], F32, kind="ExternalInput").ap()
    wod = nc.dram_tensor("wo", [D, D], F32, kind="ExternalInput").ap()
    cosd = nc.dram_tensor("cos_t", [128, S], F32, kind="ExternalInput").ap()
    sind = nc.dram_tensor("sin_t", [128, S], F32, kind="ExternalInput").ap()
    gwd = nc.dram_tensor("gate_w", [D, NE], F32, kind="ExternalInput").ap()
    w1d = nc.dram_tensor("w1p", [NE * 8 * 128, 4096], F32, kind="ExternalInput").ap()
    w2d = nc.dram_tensor("w2p", [NE * 8 * 128, 4096], F32, kind="ExternalInput").ap()
    b1d = nc.dram_tensor("b1r", [128, 256], F32, kind="ExternalInput").ap()
    b2d = nc.dram_tensor("b2", [NE, D], F32, kind="ExternalInput").ap()
    gammad = nc.dram_tensor("gamma", [1, D], F32, kind="ExternalInput").ap()
    betad = nc.dram_tensor("beta", [1, D], F32, kind="ExternalInput").ap()
    identbd = nc.dram_tensor("identb", [128, 128], BF16, kind="ExternalInput").ap()
    identfd = nc.dram_tensor("identf", [128, 128], F32, kind="ExternalInput").ap()
    permtd = nc.dram_tensor("permT", [128, 128], BF16, kind="ExternalInput").ap()
    onescd = nc.dram_tensor("ones_c", [128, 1], BF16, kind="ExternalInput").ap()
    onesrfd = nc.dram_tensor("ones_rf", [128, 128], F32, kind="ExternalInput").ap()
    yd = nc.dram_tensor("y", [TPC, D], F32, kind="ExternalOutput").ap()

    dbg = {}
    if debug >= 1:
        dbg["x0T"] = nc.dram_tensor("dbg_x0T", [128, 8 * S], BF16, kind="ExternalOutput").ap()
        dbg["x0Tb"] = nc.dram_tensor("dbg_x0Tb", [128, 8 * S], BF16, kind="ExternalOutput").ap()
        dbg["x0Tc"] = nc.dram_tensor("dbg_x0Tc", [128, 8 * S], BF16, kind="ExternalOutput").ap()
    if debug >= 2:
        dbg["q0"] = nc.dram_tensor("dbg_q0", [128, TPC], BF16, kind="ExternalOutput").ap()
        dbg["k0"] = nc.dram_tensor("dbg_k0", [128, S], BF16, kind="ExternalOutput").ap()
        dbg["v0"] = nc.dram_tensor("dbg_v0", [128, S], BF16, kind="ExternalOutput").ap()
        dbg["attnT"] = nc.dram_tensor("dbg_attnT", [128, 8 * TPC], BF16, kind="ExternalOutput").ap()
    if debug >= 3:
        dbg["x1"] = nc.dram_tensor("dbg_x1", [128, NT * D], F32, kind="ExternalOutput").ap()
        dbg["wot"] = nc.dram_tensor("dbg_wot", [128, 8 * D], BF16, kind="ExternalOutput").ap()
        dbg["z0"] = nc.dram_tensor("dbg_z0", [128, D], F32, kind="ExternalOutput").ap()
        dbg["st0"] = nc.dram_tensor("dbg_st0", [128, 16], F32, kind="ExternalOutput").ap()
    if debug >= 4:
        dbg["xnT"] = nc.dram_tensor("dbg_xnT", [128, 8 * TPC], BF16, kind="ExternalOutput").ap()
        dbg["cw"] = nc.dram_tensor("dbg_cw", [128, 128], F32, kind="ExternalOutput").ap()
    if debug >= 5:
        dbg["hT"] = nc.dram_tensor("dbg_hT", [128, 32 * 512], BF16, kind="ExternalOutput").ap()

    p = Prog()
    B = lambda: Buf()

    st0 = ExitStack()
    sbuf = lambda stk, name, shape, dt, **kw: stk.enter_context(nc.sbuf_tensor(name, shape, dt, **kw))
    psumt = lambda stk, name, shape, dt: stk.enter_context(nc.psum_tensor(name, shape, dt))

    consts = sbuf(st0, "consts", [128, 8], F32)
    identb = sbuf(st0, "identb_s", [128, 128], BF16)
    identf = sbuf(st0, "identf_s", [128, 128], F32)
    permt = sbuf(st0, "permt_s", [128, 128], BF16)
    onesc = sbuf(st0, "onesc_s", [128, 1], BF16)
    onesrf = sbuf(st0, "onesrf_s", [128, 128], F32)
    b1sb = sbuf(st0, "b1sb", [128, 256], F32)
    bnbuf = sbuf(st0, "bnbuf", [128, 24], F32)
    stats2 = sbuf(st0, "stats2", [128, 16], F32)
    gsb = sbuf(st0, "gsb", [128, 64], F32)
    b2sb = sbuf(st0, "b2sb", [8, D], BF16)
    stats = sbuf(st0, "stats", [128, 16], F32)

    b_consts, b_identb, b_identf, b_permt, b_onesc, b_onesrf = B(), B(), B(), B(), B(), B()
    b_gsm, b_bsm, b_gbc, b_bbc, b_b1, b_gsb, b_b2 = B(), B(), B(), B(), B(), B(), B()
    b_cwtm, b_cwb, b_cwT, b_stats, b_stats2 = B(), B(), B(), B(), B()

    psA = ExitStack()
    pf = [psumt(psA, f"pf{i}", [128, 512], F32) for i in range(6)]
    pb = [psumt(psA, f"pb{i}", [128, 1024], BF16) for i in range(2)]
    b_pf = [B() for _ in range(6)]
    b_pb = [B() for _ in range(2)]

    p.step("v", [lambda e: e.memset(consts[:, 0:1], SHIFT),
                 lambda e: e.memset(consts[:, 1:2], EPS)], writes=[b_consts])
    p.step("y", lambda e: e.dma_start(identb[:, :], identbd[:, :]), writes=[b_identb], dma=True, sem="ycst")
    p.step("y", lambda e: e.dma_start(identf[:, :], identfd[:, :]), writes=[b_identf], dma=True, sem="ycst")
    p.step("y", lambda e: e.dma_start(permt[:, :], permtd[:, :]), writes=[b_permt], dma=True, sem="ycst")
    p.step("y", lambda e: e.dma_start(onesc[:, :], onescd[:, :]), writes=[b_onesc], dma=True, sem="ycst")
    p.step("y", lambda e: e.dma_start(onesrf[:, :], onesrfd[:, :]), writes=[b_onesrf], dma=True, sem="ycst")
    p.step("y", lambda e: e.dma_start(b1sb[:, :], b1d[:, :]), writes=[b_b1], dma=True, sem="ycst")
    p.step("y", [(lambda e, dc=dc: e.dma_start(gsb[:, dc * 8:(dc + 1) * 8],
                                               gwd[dc * 128:(dc + 1) * 128, :]))
                 for dc in range(8)], writes=[b_gsb], dma=True, sem="ycst")
    p.step("g", lambda e: e.dma_start(b2sb[:, :], b2d[:, :]), writes=[b_b2], dma=True, sem="gc")

    stA = ExitStack()
    cossb = sbuf(stA, "cossb", [128, S], F32)
    sinsb = sbuf(stA, "sinsb", [128, S], F32)
    x0T = sbuf(stA, "x0T", [128, 8 * S], BF16)
    wst = [sbuf(stA, f"wst{i}", [128, D], BF16) for i in range(2)]
    wqt = sbuf(stA, "wqt", [128, D], BF16)
    wkt = sbuf(stA, "wkt", [128, D], BF16)
    wvt = sbuf(stA, "wvt", [128, D], BF16)
    Q2 = [sbuf(stA, f"Q2{i}", [128, TPC], BF16) for i in range(2)]
    K2 = [sbuf(stA, f"K2{i}", [128, S], BF16) for i in range(2)]
    V2 = [sbuf(stA, f"V2{i}", [128, S], BF16) for i in range(2)]
    praw = [sbuf(stA, f"praw{i}", [128, 512], BF16) for i in range(2)]
    rt1 = sbuf(stA, "rt1", [128, 512], F32)
    rt2 = sbuf(stA, "rt2", [128, 512], F32)
    ptA = [sbuf(stA, f"ptA{i}", [128, 512], BF16) for i in range(2)]
    ptB = [sbuf(stA, f"ptB{i}", [128, 512], BF16) for i in range(2)]
    zs = sbuf(stA, "zs", [33, 512], F32)
    zbs = sbuf(stA, "zbs", [128, 512], F32)

    b_cos, b_sin, b_x0T = B(), B(), B()
    b_x0Tw = [B() for _ in range(8)]
    b_wst = [B() for _ in range(2)]
    b_wqt, b_wkt, b_wvt = B(), B(), B()
    b_Q2 = [B() for _ in range(2)]
    b_K2 = [B() for _ in range(2)]
    b_V2 = [B() for _ in range(2)]
    b_praw = [B() for _ in range(2)]
    b_rt1, b_rt2 = B(), B()
    b_ptA = [B() for _ in range(2)]
    b_ptB = [B() for _ in range(2)]
    b_zs, b_zbs = B(), B()

    stAttnT = ExitStack()
    attnT = sbuf(stAttnT, "attnT", [128, 8 * TPC], BF16, side="right")
    b_attnT = B()

    p.step("y", lambda e: e.dma_start(cossb[:, :], cosd[:, :]), writes=[b_cos], dma=True, sem="ycst")
    p.step("y", lambda e: e.dma_start(sinsb[:, :], sind[:, :]), writes=[b_sin], dma=True, sem="ycst")
    ycst_tok = ("ycst", p.cnt["ycst"])
    for _b in (b_identb, b_identf, b_permt, b_onesc, b_onesrf, b_b1, b_gsb,
               b_cos, b_sin):
        _b.w = ycst_tok

    stA2 = ExitStack()
    x0tm = [sbuf(stA2, f"x0tm{i}", [128, D], BF16) for i in range(2)]
    b_x0tm = [B() for _ in range(2)]
    for i in range(NTF):
        bx = b_x0tm[i % 2]
        p.step("g", (lambda e, i=i, t=x0tm[i % 2]: e.dma_start(
            t[:, :], xd[i * 128:(i + 1) * 128, :])), writes=[bx], dma=True,
            sem=f"ga{i % 2}")
        bp = b_pb[i % 2]
        p.step("t", [(lambda e, dc=dc, i=i, t=x0tm[i % 2]: e.transpose(
            pb[i % 2][:, dc * 128:(dc + 1) * 128], t[:, dc * 128:(dc + 1) * 128],
            identb[:, :])) for dc in range(8)],
            reads=[bx, b_identb], writes=[bp])
        p.step("v", [(lambda e, dc=dc, i=i: e.tensor_copy(
            x0T[:, dc * S + i * 128: dc * S + (i + 1) * 128],
            pb[i % 2][:, dc * 128:(dc + 1) * 128])) for dc in range(8)],
            reads=[bp], writes=[b_x0T, b_x0Tw[i // 4]])

    if debug >= 1:
        p.step("y", lambda e: e.dma_start(dbg["x0T"][:, :], x0T[:, :]),
               reads=[b_x0T], dma=True, sem="ydmp")

    p.barrier()
    stA2.close()
    psA.close()
    psB = ExitStack()
    pst = [psumt(psB, f"pst{i}", [128, 512], F32) for i in range(4)]
    ppv = psumt(psB, "ppv", [128, 512], F32)
    ppz = psumt(psB, "ppz", [128, 512], F32)
    pgp = psumt(psB, "pgp", [128, 512], F32)
    pgm = psumt(psB, "pgm", [128, 512], F32)
    b_pst = [B() for _ in range(4)]
    b_ppv, b_ppz, b_pgp, b_pgm = B(), B(), B(), B()

    thunks = []

    def drain(n):
        for _ in range(min(n, len(thunks))):
            thunks.pop(0)()

    def queue_prep(pair):
        sset = pair % 2
        Qs, Ks, Vs = Q2[sset], K2[sset], V2[sset]
        bQ, bK, bV = b_Q2[sset], b_K2[sset], b_V2[sset]

        def wprep(dram_ap, dst, b_dst, wring, pair=pair):
            def t1(dram_ap=dram_ap, wring=wring):
                p.step("g", (lambda e, d=dram_ap, t=wst[wring]: e.dma_start(
                    t[:, :], d[pair * 128:(pair + 1) * 128, :])),
                    writes=[b_wst[wring]], dma=True, sem=f"gw{wring}")
            def t2(dst=dst, b_dst=b_dst, wring=wring):
                p.step("y", [(lambda e, dc=dc, t=wst[wring], dst=dst:
                              e.dma_start_transpose(
                    dst[:, dc * 128:(dc + 1) * 128], t[:, dc * 128:(dc + 1) * 128]))
                    for dc in range(8)],
                    reads=[b_wst[wring]], writes=[b_dst], dma=True, sem="ywt")
                b_dst.w = ("ywt", p.cnt["ywt"])
            thunks.append(t1)
            thunks.append(t2)

        wprep(wqd, wqt, b_wqt, 0)
        wprep(wkd, wkt, b_wkt, 1)
        wprep(wvd, wvt, b_wvt, 0)

        def projgrp(gg, wt, b_wt, dst, b_dst, scaled):
            def th_b(gg=gg, wt=wt, b_wt=b_wt, scaled=scaled):
                p.step("t", [(lambda e, dc=dc, gg=gg, wt=wt: e.matmul(
                    pgp[:, :], wt[:, dc * 128:(dc + 1) * 128],
                    x0T[:, dc * S + gg * 512: dc * S + (gg + 1) * 512],
                    start=(dc == 0), stop=(dc == 7))) for dc in range(8)],
                    reads=[b_wt, b_x0Tw[gg]], writes=[b_pgp])
                pr, bpr = praw[gg % 2], b_praw[gg % 2]
                sc = 0.125 if scaled else 1.0
                p.step("v", (lambda e, pr=pr, sc=sc: e.tensor_scalar(
                    pr[:, :], pgp[:, :], sc, None, OP.mult)),
                    reads=[b_pgp], writes=[bpr])
                p.fence("v")
                p.step("v", (lambda e, gg=gg, sc=sc: e.scalar_tensor_tensor(
                    rt1[:, :], pgp[:, :], sc, cossb[:, gg * 512:(gg + 1) * 512],
                    OP.mult, OP.mult)),
                    reads=[b_pgp, b_cos], writes=[b_rt1])
            def th_c(gg=gg, dst=dst, b_dst=b_dst, scaled=scaled):
                pr, bpr = praw[gg % 2], b_praw[gg % 2]
                p.step("t", (lambda e, pr=pr: e.matmul(
                    pgm[:, :], permt[:, :], pr[:, :], start=True, stop=True)),
                    reads=[b_permt, bpr], writes=[b_pgm])
                p.step("v", (lambda e, gg=gg: e.tensor_tensor(
                    rt2[:, :], pgm[:, :], sinsb[:, gg * 512:(gg + 1) * 512], OP.mult)),
                    reads=[b_pgm, b_sin], writes=[b_rt2])
                p.fence("v")
                p.step("v", (lambda e, gg=gg, dst=dst: e.tensor_tensor(
                    dst[:, gg * 512:(gg + 1) * 512], rt1[:, :], rt2[:, :], OP.add)),
                    reads=[b_rt1, b_rt2], writes=[b_dst])
            thunks.extend([th_b, th_c])

        def vtile(vt):
            q = vt % 4
            def th(vt=vt, q=q):
                if q == 0:
                    p.pre_write("t", b_pgm)
                tok = p.step("t", [(lambda e, dc=dc, vt=vt, q=q: e.matmul(
                    pgm[:, q * 128:(q + 1) * 128],
                    x0T[:, dc * S + vt * 128: dc * S + (vt + 1) * 128],
                    wvt[:, dc * 128:(dc + 1) * 128],
                    start=(dc == 0), stop=(dc == 7))) for dc in range(8)],
                    reads=[b_x0Tw[vt // 4], b_wvt], writes=[])
                if q == 3:
                    p.set_write(tok, b_pgm)
                    gv = vt // 4
                    p.step("v", (lambda e, gv=gv, Vs=Vs: e.tensor_copy(
                        Vs[:, gv * 512:(gv + 1) * 512], pgm[:, :])),
                        reads=[b_pgm], writes=[bV])
            thunks.append(th)

        for gg in range(TPC // 512):
            projgrp(gg, wqt, b_wqt, Qs, bQ, True)
        for gg in range(S // 512):
            projgrp(gg, wkt, b_wkt, Ks, bK, False)
        for vt in range(NTF):
            vtile(vt)

    queue_prep(0)
    drain(1000)

    for pair in range(NPAIR):
        sset = pair % 2
        Qs, Ks, Vs = Q2[sset], K2[sset], V2[sset]
        bQ, bK, bV = b_Q2[sset], b_K2[sset], b_V2[sset]
        if pair + 1 < NPAIR:
            queue_prep(pair + 1)
        if debug >= 1 and pair == 4:
            p.step("y", lambda e: e.dma_start(dbg["x0Tb"][:, :], x0T[:, :]),
                   reads=[b_x0T], dma=True, sem="ydmp")
        if debug >= 2 and pair == 0:
            p.step("y", lambda e: e.dma_start(dbg["q0"][:, :], Q2[0][:, :]), reads=[bQ], dma=True, sem="ydmp")
            p.step("y", lambda e: e.dma_start(dbg["k0"][:, :], K2[0][:, :]), reads=[bK], dma=True, sem="ydmp")
            p.step("y", lambda e: e.dma_start(dbg["v0"][:, :], V2[0][:, :]), reads=[bV], dma=True, sem="ydmp")

        for g in range(TPC // 512):
            def emit_st(c, g=g, Ks=Ks, Qs=Qs, bK=bK, bQ=bQ):
                i2 = c % 2
                p.step("t", [
                    (lambda e, c=c, g=g, i=i2: e.matmul(
                        pst[i][:, :], Ks[0:64, c * 128:(c + 1) * 128],
                        Qs[0:64, g * 512:(g + 1) * 512], start=True, stop=True)),
                    (lambda e, c=c, g=g, i=2 + i2: e.matmul(
                        pst[i][:, :], Ks[64:128, c * 128:(c + 1) * 128],
                        Qs[64:128, g * 512:(g + 1) * 512], start=True, stop=True))],
                    reads=[bK, bQ], writes=[b_pst[i2], b_pst[2 + i2]])
                p.step("s", (lambda e, c=c, i=i2: e.activation(
                    ptA[c % 2][:, :], pst[i][:, :], AF.Exp, bias=consts[:, 0:1], scale=1.0)),
                    reads=[b_pst[i2], b_consts], writes=[b_ptA[c % 2]])
                p.step("s", (lambda e, c=c, i=2 + i2: e.activation(
                    ptB[c % 2][:, :], pst[i][:, :], AF.Exp, bias=consts[:, 0:1], scale=1.0)),
                    reads=[b_pst[2 + i2], b_consts], writes=[b_ptB[c % 2]])

            def emit_pvz(c, Vs=Vs, bV=bV):
                return p.step("t", [
                    (lambda e, c=c: e.matmul(
                        ppv[0:64, :], Vs[:, c * 128: c * 128 + 64], ptA[c % 2][:, :],
                        start=(c == 0), stop=(c == NTF - 1), tile_position=(0, 0),
                        skip_group_check=True)),
                    (lambda e, c=c: e.matmul(
                        ppv[64:128, :], Vs[:, c * 128 + 64: c * 128 + 128],
                        ptB[c % 2][:, :],
                        start=(c == 0), stop=(c == NTF - 1), tile_position=(0, 64),
                        skip_group_check=True)),
                    (lambda e, c=c: e.matmul(
                        ppz[0:1, :], onesc[:, :], ptA[c % 2][:, :],
                        start=(c == 0), stop=(c == NTF - 1), tile_position=(0, 0),
                        skip_group_check=True)),
                    (lambda e, c=c: e.matmul(
                        ppz[32:33, :], onesc[:, :], ptB[c % 2][:, :],
                        start=(c == 0), stop=(c == NTF - 1), tile_position=(0, 32),
                        skip_group_check=True))],
                    reads=[bV, b_ptA[c % 2], b_ptB[c % 2], b_onesc], writes=[])

            for c in range(NTF):
                emit_st(c)
                drain(1)
                if c == 1:
                    p.pre_write("t", b_ppv, b_ppz)
                if c >= 1:
                    tok = emit_pvz(c - 1)
            tok = emit_pvz(NTF - 1)
            p.set_write(tok, b_ppv, b_ppz)
            p.step("v", [lambda e: e.tensor_copy(zs[0:1, :], ppz[0:1, :]),
                         lambda e: e.tensor_copy(zs[32:33, :], ppz[32:33, :])],
                   reads=[b_ppz], writes=[b_zs])
            p.step("s", (lambda e: e.activation(rt2[0:33, :], zs[0:33, :], AF.Ln)),
                   reads=[b_zs], writes=[b_rt2])
            p.fence("s")
            p.step("s", (lambda e: e.activation(zs[0:33, :], rt2[0:33, :], AF.Exp, scale=-1.0)),
                   reads=[b_rt2], writes=[b_zs])
            p.step("t", [
                lambda e: e.matmul(pgp[0:64, :], onesrf[0:1, 0:64], zs[0:1, :],
                                   start=True, stop=True, tile_position=(0, 0)),
                lambda e: e.matmul(pgp[64:128, :], onesrf[32:33, 0:64], zs[32:33, :],
                                   start=True, stop=True, tile_position=(32, 64))],
                reads=[b_onesrf, b_zs], writes=[b_pgp])
            p.step("v", lambda e: e.tensor_copy(zbs[:, :], pgp[:, :]),
                   reads=[b_pgp], writes=[b_zbs])
            p.fence("v")
            p.step("v", [
                (lambda e, pair=pair, g=g: e.tensor_tensor(
                    attnT[0:64, pair * TPC + g * 512: pair * TPC + (g + 1) * 512],
                    ppv[0:64, :], zbs[0:64, :], OP.mult)),
                (lambda e, pair=pair, g=g: e.tensor_tensor(
                    attnT[64:128, pair * TPC + g * 512: pair * TPC + (g + 1) * 512],
                    ppv[64:128, :], zbs[64:128, :], OP.mult))],
                reads=[b_ppv, b_zbs], writes=[b_attnT])
        drain(1000)

    if debug >= 1:
        p.step("y", lambda e: e.dma_start(dbg["x0Tc"][:, :], x0T[:, :]),
               reads=[b_x0T], dma=True, sem="ydmp")
    if debug >= 2:
        p.step("y", lambda e: e.dma_start(dbg["attnT"][:, :], attnT[:, :]),
               reads=[b_attnT], dma=True, sem="ydmp")

    p.barrier()
    psB.close()
    psC = ExitStack()
    pf = [psumt(psC, f"pfc{i}", [128, 512], F32) for i in range(6)]
    pb = [psumt(psC, f"pbc{i}", [128, 1024], BF16) for i in range(2)]
    b_pf = [B() for _ in range(6)]
    b_pb = [B() for _ in range(2)]

    p.barrier()
    stA.close()
    stX = ExitStack()
    x1 = sbuf(stX, "x1", [128, NT * D], BF16)
    b_x1 = B()
    stC = ExitStack()
    wot = sbuf(stC, "wot", [128, 8 * D], BF16)
    wstC = [sbuf(stC, f"wstC{i}", [128, D], BF16) for i in range(2)]
    x0re = [sbuf(stC, f"x0re{i}", [128, D], F32) for i in range(2)]
    zC = [sbuf(stC, f"zC{i}", [128, D], F32) for i in range(2)]
    b_wot = B()
    b_wstC = [B() for _ in range(2)]
    b_x0re = [B() for _ in range(2)]
    b_zC = [B() for _ in range(2)]

    for nchunk in range(8):
        bw = b_wstC[nchunk % 2]
        p.step("g", (lambda e, n=nchunk, t=wstC[nchunk % 2]: e.dma_start(
            t[:, :], wod[n * 128:(n + 1) * 128, :])), writes=[bw], dma=True,
            sem=f"gwc{nchunk % 2}")
        bp = b_pb[nchunk % 2]
        p.step("t", [(lambda e, dc=dc, n=nchunk, t=wstC[nchunk % 2]: e.transpose(
            pb[n % 2][:, dc * 128:(dc + 1) * 128], t[:, dc * 128:(dc + 1) * 128],
            identb[:, :])) for dc in range(8)], reads=[bw, b_identb], writes=[bp])
        p.step("v", [(lambda e, dc=dc, n=nchunk: e.tensor_copy(
            wot[:, dc * D + n * 128: dc * D + (n + 1) * 128],
            pb[n % 2][:, dc * 128:(dc + 1) * 128])) for dc in range(8)],
            reads=[bp], writes=[b_wot])

    def ln_stats(ten, c0, b_z, rvar_col):
        del rvar_col
        p.fence("v")
        p.step("v", [
            (lambda e: e.bn_stats(bnbuf[:, 0:6], ten[:, c0:c0 + 512])),
            (lambda e: e.bn_stats(bnbuf[:, 6:12], ten[:, c0 + 512:c0 + 1024]))],
            reads=[b_z], writes=[b_stats])
        p.fence("v")
        p.step("v", (lambda e: e.bn_aggr(stats[:, 2:4], bnbuf[:, 0:12])),
               reads=[b_z], writes=[b_stats])
        p.step("s", (lambda e: e.activation(stats[:, 5:6], stats[:, 3:4], AF.Ln,
                                            bias=consts[:, 1:2])),
               reads=[b_stats, b_consts], writes=[b_stats])
        p.fence("s")
        p.step("s", (lambda e: e.activation(stats[:, 10:11], stats[:, 5:6], AF.Exp,
                                            scale=-0.5)),
               reads=[b_stats], writes=[b_stats])
        p.step("v", (lambda e: e.tensor_tensor(stats[:, 4:5], stats[:, 2:3],
                                               stats[:, 10:11], OP.mult)),
               reads=[b_stats], writes=[b_stats])
        p.fence("v")
        p.step("v", (lambda e: e.tensor_scalar(stats[:, 11:12], stats[:, 4:5],
                                               -1.0, None, OP.mult)),
               reads=[b_stats], writes=[b_stats])
        p.fence("v")

    ln_pair_helper_marker = True
    def ln_stats_pair(tenA, cA, bzA, tenB, cB, bzB):
        p.fence("v")
        p.step("v", [
            (lambda e: e.bn_stats(bnbuf[:, 0:6], tenA[:, cA:cA + 512])),
            (lambda e: e.bn_stats(bnbuf[:, 6:12], tenA[:, cA + 512:cA + 1024]))],
            reads=[bzA], writes=[b_stats])
        p.step("v", [
            (lambda e: e.bn_stats(bnbuf[:, 12:18], tenB[:, cB:cB + 512])),
            (lambda e: e.bn_stats(bnbuf[:, 18:24], tenB[:, cB + 512:cB + 1024]))],
            reads=[bzB], writes=[b_stats2])
        p.fence("v")
        p.step("v", (lambda e: e.bn_aggr(stats[:, 2:4], bnbuf[:, 0:12])),
               reads=[bzA], writes=[b_stats])
        p.step("v", (lambda e: e.bn_aggr(stats2[:, 2:4], bnbuf[:, 12:24])),
               reads=[bzB], writes=[b_stats2])
        p.step("s", (lambda e: e.activation(stats[:, 5:6], stats[:, 3:4], AF.Ln,
                                            bias=consts[:, 1:2])),
               reads=[b_stats, b_consts], writes=[b_stats])
        p.step("s", (lambda e: e.activation(stats2[:, 5:6], stats2[:, 3:4], AF.Ln,
                                            bias=consts[:, 1:2])),
               reads=[b_stats2, b_consts], writes=[b_stats2])
        p.fence("s")
        p.step("s", (lambda e: e.activation(stats[:, 10:11], stats[:, 5:6], AF.Exp,
                                            scale=-0.5)), reads=[b_stats], writes=[b_stats])
        p.step("s", (lambda e: e.activation(stats2[:, 10:11], stats2[:, 5:6], AF.Exp,
                                            scale=-0.5)), reads=[b_stats2], writes=[b_stats2])
        p.step("v", (lambda e: e.tensor_tensor(stats[:, 4:5], stats[:, 2:3],
                                               stats[:, 10:11], OP.mult)),
               reads=[b_stats], writes=[b_stats])
        p.step("v", (lambda e: e.tensor_tensor(stats2[:, 4:5], stats2[:, 2:3],
                                               stats2[:, 10:11], OP.mult)),
               reads=[b_stats2], writes=[b_stats2])
        p.fence("v")
        p.step("v", (lambda e: e.tensor_scalar(stats[:, 11:12], stats[:, 4:5],
                                               -1.0, None, OP.mult)),
               reads=[b_stats], writes=[b_stats])
        p.step("v", (lambda e: e.tensor_scalar(stats2[:, 11:12], stats2[:, 4:5],
                                               -1.0, None, OP.mult)),
               reads=[b_stats2], writes=[b_stats2])
        p.fence("v")

    for tt0 in range(0, NT, 2):
        zz = {}
        for tt in (tt0, tt0 + 1):
            bx0 = b_x0re[tt % 2]
            p.step("y", (lambda e, tt=tt, t=x0re[tt % 2]: e.dma_start(
                t[:, :], xd[tt * 128:(tt + 1) * 128, :])), writes=[bx0], dma=True,
                sem=f"yre{tt % 2}")
            bL, bH = b_pf[(tt % 2) * 2], b_pf[(tt % 2) * 2 + 1]
            mms = []
            for dc in range(8):
                mms.append(lambda e, dc=dc, tt=tt, i=(tt % 2) * 2: e.matmul(
                    pf[i][:, :], attnT[:, dc * TPC + tt * 128: dc * TPC + (tt + 1) * 128],
                    wot[:, dc * D: dc * D + 512], start=(dc == 0), stop=(dc == 7)))
                mms.append(lambda e, dc=dc, tt=tt, i=(tt % 2) * 2 + 1: e.matmul(
                    pf[i][:, :], attnT[:, dc * TPC + tt * 128: dc * TPC + (tt + 1) * 128],
                    wot[:, dc * D + 512: dc * D + 1024], start=(dc == 0), stop=(dc == 7)))
            p.step("t", mms, reads=[b_attnT, b_wot], writes=[bL, bH])
            bz = b_zC[tt % 2]
            p.step("v", [
                (lambda e, tt=tt, i=(tt % 2) * 2, z=zC[tt % 2]: e.tensor_tensor(
                    z[:, 0:512], pf[i][:, :], x0re[tt % 2][:, 0:512], OP.add)),
                (lambda e, tt=tt, i=(tt % 2) * 2 + 1, z=zC[tt % 2]: e.tensor_tensor(
                    z[:, 512:1024], pf[i][:, :], x0re[tt % 2][:, 512:1024], OP.add))],
                reads=[bL, bH, bx0], writes=[bz])
            zz[tt] = bz
        ln_stats_pair(zC[tt0 % 2], 0, zz[tt0], zC[(tt0 + 1) % 2], 0, zz[tt0 + 1])
        p.step("v", (lambda e, tt=tt0, z=zC[tt0 % 2]: e.tensor_scalar(
            x1[:, tt * D:(tt + 1) * D], z[:, :],
            stats[:, 10:11], stats[:, 11:12], OP.mult, OP.add)),
            reads=[zz[tt0], b_stats], writes=[b_x1])
        p.step("v", (lambda e, tt=tt0 + 1, z=zC[(tt0 + 1) % 2]: e.tensor_scalar(
            x1[:, tt * D:(tt + 1) * D], z[:, :],
            stats2[:, 10:11], stats2[:, 11:12], OP.mult, OP.add)),
            reads=[zz[tt0 + 1], b_stats2], writes=[b_x1])

    if debug >= 3:
        p.step("y", lambda e: e.dma_start(dbg["x1"][:, :], x1[:, :]), reads=[b_x1], dma=True, sem="ydmp")
        p.step("y", lambda e: e.dma_start(dbg["wot"][:, :], wot[:, :]), reads=[b_wot], dma=True, sem="ydmp")

    p.barrier()
    stC.close()
    stAttnT.close()
    stD = ExitStack()
    xnT = sbuf(stD, "xnT", [128, 8 * TPC], BF16)
    cw_tm = sbuf(stD, "cw_tm", [128, 128], F32)
    cwb = sbuf(stD, "cwb", [128, 128], BF16)
    cwT = sbuf(stD, "cwT", [8, TPC], BF16)
    b_xnT = B()
    stD2 = ExitStack()
    gsm = sbuf(stD2, "gsm", [1, D], F32)
    bsm = sbuf(stD2, "bsm", [1, D], F32)
    gbc = sbuf(stD2, "gbc", [128, D], F32)
    bbc = sbuf(stD2, "bbc", [128, D], F32)
    xcD = [sbuf(stD2, f"xcD{i}", [128, D], F32) for i in range(2)]
    xnD = [sbuf(stD2, f"xnD{i}", [128, D], F32) for i in range(2)]
    xnT32 = [sbuf(stD2, f"xnT32{i}", [128, D], F32) for i in range(2)]
    lgD = sbuf(stD2, "lgD", [128, 48], F32)
    b_xcD = [B() for _ in range(2)]
    b_xnD = [B() for _ in range(2)]
    b_xnT32 = [B() for _ in range(2)]
    b_lgD = B()

    p.step("y", lambda e: e.dma_start(gsm[:, :], gammad[:, :]), writes=[b_gsm], dma=True, sem="ygb")
    p.step("y", lambda e: e.dma_start(bsm[:, :], betad[:, :]), writes=[b_bsm], dma=True, sem="ygb")
    ygb_tok = ("ygb", p.cnt["ygb"])
    b_gsm.w = ygb_tok
    b_bsm.w = ygb_tok
    for src, bsrc, dst, bdst in ((gsm, b_gsm, gbc, b_gbc), (bsm, b_bsm, bbc, b_bbc)):
        p.step("t", [(lambda e, h=h, src=src: e.matmul(
            pf[h][:, :], onesrf[0:1, 0:128], src[0:1, h * 512:(h + 1) * 512],
            start=True, stop=True)) for h in range(2)],
            reads=[b_onesrf, bsrc], writes=[b_pf[0], b_pf[1]])
        p.step("v", [(lambda e, h=h, dst=dst: e.tensor_copy(
            dst[:, h * 512:(h + 1) * 512], pf[h][:, :])) for h in range(2)],
            reads=[b_pf[0], b_pf[1]], writes=[bdst])

    for tt in range(NT):
        ln_stats(x1, tt * D, b_x1, 8)
        bxc = b_xcD[tt % 2]
        p.step("v", (lambda e, tt=tt, t=xcD[tt % 2]: e.tensor_scalar(
            t[:, :], x1[:, tt * D:(tt + 1) * D],
            stats[:, 10:11], stats[:, 11:12], OP.mult, OP.add)),
            reads=[b_x1, b_stats], writes=[bxc])
        bxn = b_xnD[tt % 2]
        p.fence("v")
        p.step("v", (lambda e, t=xcD[tt % 2], o=xnD[tt % 2]: e.tensor_tensor(
                o[:, :], t[:, :], gbc[:, :], OP.mult)),
            reads=[bxc, b_gbc], writes=[bxn])
        p.fence("v")
        p.step("v", (lambda e, o=xnD[tt % 2]: e.tensor_tensor(
                o[:, :], o[:, :], bbc[:, :], OP.add)),
            reads=[b_bbc], writes=[bxn])
        bq0, bq1 = b_pf[tt % 2], b_pf[2 + tt % 2]
        p.step("t", [(lambda e, j=j, tt=tt, t=xnD[tt % 2], i=tt % 2: e.transpose(
            pf[i][:, j * 128:(j + 1) * 128], t[:, j * 128:(j + 1) * 128], identf[:, :]))
            for j in range(4)], reads=[bxn, b_identf], writes=[bq0])
        p.step("t", [(lambda e, j=j, tt=tt, t=xnD[tt % 2], i=2 + tt % 2: e.transpose(
            pf[i][:, (j - 4) * 128:(j - 3) * 128], t[:, j * 128:(j + 1) * 128],
            identf[:, :])) for j in range(4, 8)], reads=[bxn, b_identf], writes=[bq1])
        evs = []
        for dc in range(8):
            i = (tt % 2) if dc < 4 else (2 + tt % 2)
            off = (dc % 4) * 128
            evs.append(lambda e, dc=dc, tt=tt, i=i, off=off: e.tensor_copy(
                xnT[:, dc * TPC + tt * 128: dc * TPC + (tt + 1) * 128],
                pf[i][:, off:off + 128]))
        p.step("v", evs, reads=[bq0, bq1], writes=[b_xnT])
        bx32 = b_xnT32[tt % 2]
        p.step("v", [
            (lambda e, tt=tt, i=tt % 2, t=xnT32[tt % 2]: e.tensor_copy(
                t[:, 0:512], pf[i][:, :])),
            (lambda e, tt=tt, i=2 + tt % 2, t=xnT32[tt % 2]: e.tensor_copy(
                t[:, 512:1024], pf[i][:, :]))],
            reads=[bq0, bq1], writes=[bx32])
        bg = b_pf[4]
        p.step("t", [(lambda e, dc=dc, tt=tt, t=xnT32[tt % 2]: e.matmul(
            pf[4][:, 0:8], t[:, dc * 128:(dc + 1) * 128], gsb[:, dc * 8:(dc + 1) * 8],
            start=(dc == 0), stop=(dc == 7))) for dc in range(8)],
            reads=[bx32, b_gsb], writes=[bg])
        c0, c1, c2, c3, c4 = (lgD[:, 0:8], lgD[:, 8:16], lgD[:, 16:24],
                              lgD[:, 24:32], lgD[:, 32:40])
        m1, m2, ssum = lgD[:, 40:41], lgD[:, 41:42], lgD[:, 42:43]
        p.step("v", (lambda e: e.reduce_max(m1, pf[4][:, 0:8], AXX)),
            reads=[bg], writes=[b_lgD])
        p.fence("v")
        p.step("v", (lambda e: e.tensor_scalar(c0, pf[4][:, 0:8], m1, None, OP.subtract)),
            reads=[bg], writes=[b_lgD])
        p.step("s", (lambda e: e.activation(c1, c0, AF.Exp)), reads=[b_lgD], writes=[b_lgD])
        p.fence("v")
        p.step("v", (lambda e: e.tensor_scalar(c2, c0, 0.0, None, OP.is_ge)),
            reads=[b_lgD], writes=[b_lgD])
        p.fence("v")
        p.step("v", (lambda e: e.tensor_scalar(c2, c2, -1e30, None, OP.mult)),
            reads=[b_lgD], writes=[b_lgD])
        p.fence("v")
        p.step("v", (lambda e: e.tensor_tensor(c2, c0, c2, OP.add)),
            reads=[b_lgD], writes=[b_lgD])
        p.fence("v")
        p.step("v", (lambda e: e.reduce_max(m2, c2, AXX)),
            reads=[b_lgD], writes=[b_lgD])
        p.fence("v")
        p.step("v", (lambda e: e.tensor_scalar(c3, c0, m2, None, OP.is_ge)),
            reads=[b_lgD], writes=[b_lgD])
        p.fence("v")
        p.step("v", (lambda e: e.tensor_tensor(c4, c1, c3, OP.mult)),
            reads=[b_lgD], writes=[b_lgD])
        p.fence("v")
        p.step("v", (lambda e: e.tensor_reduce(ssum, c4, AXX, OP.add)),
            reads=[b_lgD], writes=[b_lgD])
        p.step("s", (lambda e: e.activation(lgD[:, 43:44], ssum, AF.Ln)),
            reads=[b_lgD], writes=[b_lgD])
        p.fence("s")
        p.step("s", (lambda e: e.activation(m2, lgD[:, 43:44], AF.Exp, scale=-1.0)),
            reads=[b_lgD], writes=[b_lgD])
        p.fence("v")
        p.step("v", (lambda e, tt=tt: e.tensor_scalar(
            cw_tm[:, tt * 8:(tt + 1) * 8], c4, m2, None, OP.mult)),
            reads=[b_lgD], writes=[b_cwtm])

    p.step("v", lambda e: e.tensor_copy(cwb[:, :], cw_tm[:, :]),
           reads=[b_cwtm], writes=[b_cwb])
    for half in range(2):
        bp = b_pb[half]
        p.step("t", [(lambda e, k=k, half=half: e.transpose(
            pb[half][0:8, k * 128:(k + 1) * 128],
            cwb[:, (half * 8 + k) * 8:(half * 8 + k + 1) * 8], identb[:, :]))
            for k in range(8)], reads=[b_cwb, b_identb], writes=[bp])
        p.step("v", [(lambda e, k=k, half=half: e.tensor_copy(
            cwT[0:8, (half * 8 + k) * 128:(half * 8 + k + 1) * 128],
            pb[half][0:8, k * 128:(k + 1) * 128])) for k in range(8)],
            reads=[bp], writes=[b_cwT])

    if debug >= 4:
        p.step("y", lambda e: e.dma_start(dbg["xnT"][:, :], xnT[:, :]), reads=[b_xnT], dma=True, sem="ydmp")
        p.step("y", lambda e: e.dma_start(dbg["cw"][:, :], cw_tm[:, :]), reads=[b_cwtm], dma=True, sem="ydmp")

    p.barrier()
    stD2.close()
    psC.close()
    p.barrier()
    psE = ExitStack()
    po = [psumt(psE, f"po{i}", [128, 512], F32) for i in range(8)]
    b_po = [B() for _ in range(8)]
    stE = ExitStack()
    hT = sbuf(stE, "hT", [128, 32 * 512], BF16)
    acc = sbuf(stE, "acc", [128, 4 * D], F32)
    w1b = [sbuf(stE, f"w1b{i}", [128, 4096], BF16) for i in range(3)]
    w2b = [sbuf(stE, f"w2b{i}", [128, 4096], BF16) for i in range(3)]
    tmpE = [sbuf(stE, f"tmpE{i}", [128, 512], F32) for i in range(2)]
    zE = [sbuf(stE, f"zE{i}", [128, D], F32) for i in range(2)]
    outE = [sbuf(stE, f"outE{i}", [128, D], F32) for i in range(2)]
    b_hT, b_acc = B(), B()
    b_w1b = [B() for _ in range(3)]
    b_w2b = [B() for _ in range(3)]
    b_tmpE = [B() for _ in range(2)]
    b_zE = [B() for _ in range(2)]
    b_outE = [B() for _ in range(2)]

    def emit_b2_init(tg):
        for tt in range(4):
            ttg = tg * 4 + tt
            bb0, bb1 = b_po[6], b_po[7]
            p.step("t", [
                (lambda e, ttg=ttg: e.matmul(
                    po[6][:, :], cwT[0:8, ttg * 128:(ttg + 1) * 128],
                    b2sb[0:8, 0:512], start=True, stop=True)),
                (lambda e, ttg=ttg: e.matmul(
                    po[7][:, :], cwT[0:8, ttg * 128:(ttg + 1) * 128],
                    b2sb[0:8, 512:1024], start=True, stop=True))],
                reads=[b_cwT, b_b2], writes=[bb0, bb1])
            p.step("v", [
                (lambda e, tt=tt: e.tensor_copy(acc[:, tt * D: tt * D + 512], po[6][:, :])),
                (lambda e, tt=tt: e.tensor_copy(acc[:, tt * D + 512: tt * D + 1024],
                                                po[7][:, :]))],
                reads=[bb0, bb1], writes=[b_acc])

    def emit_final(tg):
        for tt0 in (0, 2):
            zz = {}
            for tt in (tt0, tt0 + 1):
                ttg = tg * 4 + tt
                bz = b_zE[tt % 2]
                p.fence("v")
                p.step("v", (lambda e, ttg=ttg, t=zE[tt % 2]: e.tensor_scalar(
                    t[:, :], x1[:, ttg * D:(ttg + 1) * D], 2.0, None, OP.mult)),
                    reads=[b_x1], writes=[bz])
                p.fence("v")
                p.step("v", (lambda e, tt=tt, t=zE[tt % 2]: e.tensor_tensor(
                    t[:, :], t[:, :], acc[:, tt * D:(tt + 1) * D], OP.add)),
                    reads=[b_acc], writes=[bz])
                zz[tt] = bz
            ln_stats_pair(zE[tt0 % 2], 0, zz[tt0], zE[(tt0 + 1) % 2], 0, zz[tt0 + 1])
            for tt, st_t, b_st in ((tt0, stats, b_stats), (tt0 + 1, stats2, b_stats2)):
                ttg = tg * 4 + tt
                bo = b_outE[tt % 2]
                p.step("v", (lambda e, z=zE[tt % 2], o=outE[tt % 2], st_t=st_t:
                             e.tensor_scalar(
                    o[:, :], z[:, :], st_t[:, 10:11], st_t[:, 11:12], OP.mult, OP.add)),
                    reads=[zz[tt], b_st], writes=[bo])
                p.step("y", (lambda e, ttg=ttg, o=outE[tt % 2]: e.dma_start(
                    yd[ttg * 128:(ttg + 1) * 128, :], o[:, :])), reads=[bo], dma=True,
                    sem=f"yo{tt % 2}")



    for tg in range(4):
        for ex in range(NE):
            def dma_w1(js, ex=ex):
                p.step("g", (lambda e, ex=ex, js=js, t=w1b[js % 3]: e.dma_start(
                    t[:, :], w1d[(ex * 8 + js) * 128:(ex * 8 + js + 1) * 128, :])),
                    writes=[b_w1b[js % 3]], dma=True, sem=f"sw1_{js % 3}")

            def dma_w2(jg, ex=ex):
                p.step("g", (lambda e, ex=ex, jg=jg, t=w2b[jg % 3]: e.dma_start(
                    t[:, :], w2d[(ex * 8 + jg) * 128:(ex * 8 + jg + 1) * 128, :])),
                    writes=[b_w2b[jg % 3]], dma=True, sem=f"sw2_{jg % 3}")

            if ex == 0 and tg == 0:
                dma_w1(0)
                dma_w1(1)
                dma_w1(2)
            for js in range(8):
                for jj in range(4):
                    jc = js * 4 + jj
                    bh = b_po[jc % 2]
                    p.step("t", [(lambda e, dc=dc, js=js, jj=jj, tg=tg, i=jc % 2,
                                  t=w1b[js % 3]: e.matmul(
                        po[i][:, :], t[:, dc * 512 + jj * 128: dc * 512 + (jj + 1) * 128],
                        xnT[:, dc * TPC + tg * 512: dc * TPC + (tg + 1) * 512],
                        start=(dc == 0), stop=(dc == 7))) for dc in range(8)],
                        reads=[b_w1b[js % 3], b_xnT], writes=[bh])
                    p.step("s", (lambda e, jc=jc, ex=ex, i=jc % 2: e.activation(
                        hT[:, jc * 512:(jc + 1) * 512], po[i][:, :], AF.Gelu,
                        bias=b1sb[:, ex * 32 + jc: ex * 32 + jc + 1], scale=1.0)),
                        reads=[bh, b_b1], writes=[b_hT])
                if js + 3 < 8:
                    dma_w1(js + 3)
                elif js == 5:
                    dma_w2(0)
                elif js == 6:
                    dma_w2(1)
                elif js == 7:
                    dma_w2(2)
            nex, ntg = (ex + 1, tg) if ex + 1 < NE else (0, tg + 1)
            if ntg < 4:
                for sl in range(3):
                    p.step("g", (lambda e, nex=nex, sl=sl: e.dma_start(
                        w1b[sl][:, :],
                        w1d[(nex * 8 + sl) * 128:(nex * 8 + sl + 1) * 128, :])),
                        writes=[b_w1b[sl]], dma=True, sem=f"sw1_{sl}")
            if ex == 0:
                emit_b2_init(tg)
            for k in range(8):
                p.pre_write("t", b_po[k])
            tok = None
            for jg in range(8):
                for ci in range(4):
                    jc = jg * 4 + ci
                    mms = []
                    for tt in range(4):
                        for hh in range(2):
                            mms.append(lambda e, jc=jc, tt=tt, hh=hh, ci=ci,
                                       t=w2b[jg % 3]: e.matmul(
                                po[tt * 2 + hh][:, :],
                                hT[:, jc * 512 + tt * 128: jc * 512 + (tt + 1) * 128],
                                t[:, ci * 1024 + hh * 512: ci * 1024 + (hh + 1) * 512],
                                start=(jc == 0), stop=(jc == 31), skip_group_check=True))
                    tok = p.step("t", mms, reads=[b_hT, b_w2b[jg % 3]], writes=[])
                if jg + 3 < 8:
                    dma_w2(jg + 3)
            p.set_write(tok, *b_po)
            if debug >= 5 and tg == 0 and ex == 0:
                p.step("y", lambda e: e.dma_start(dbg["hT"][:, :], hT[:, :]),
                       reads=[b_hT], dma=True, sem="ydmp")
            for tt in range(4):
                ttg = tg * 4 + tt
                for hh in range(2):
                    k = tt * 2 + hh
                    btm = b_tmpE[k % 2]
                    p.step("s", (lambda e, k=k, ttg=ttg, ex=ex, t=tmpE[k % 2]: e.activation(
                        t[:, :], po[k][:, :], AF.Copy,
                        scale=cw_tm[:, ttg * 8 + ex: ttg * 8 + ex + 1])),
                        reads=[b_po[k], b_cwtm], writes=[btm])
                    p.step("v", (lambda e, tt=tt, hh=hh, k=k, t=tmpE[k % 2]: e.tensor_tensor(
                        acc[:, tt * D + hh * 512: tt * D + (hh + 1) * 512],
                        acc[:, tt * D + hh * 512: tt * D + (hh + 1) * 512],
                        t[:, :], OP.add)), reads=[btm, b_acc], writes=[b_acc])

        emit_final(tg)

    p.ops["y"].append(("w", "yo0", p.cnt["yo0"]))
    p.ops["y"].append(("w", "yo1", p.cnt["yo1"]))

    with ExitStack() as es:
        sems = {e: es.enter_context(nc.semaphore(f"sem_{e}")) for e in p.semnames}
        block = es.enter_context(nc.Block())

        def runner(ename):
            def _run(eng):
                for item in p.ops[ename]:
                    if item[0] == "w":
                        eng.wait_ge(sems[item[1]], item[2])
                    else:
                        inst = item[1](eng)
                        if item[2]:
                            inst.then_inc(sems[item[3]], item[2])
            return _run

        block.tensor(runner("t"))
        block.vector(runner("v"))
        block.scalar(runner("s"))
        block.sync(runner("y"))
        block.gpsimd(runner("g"))

    psE.close()
    stE.close()
    stD.close()
    stX.close()
    st0.close()
    return nc



def _tables():
    inv_freq = 1.0 / (10000.0 ** (np.arange(0, DK, 2, dtype=np.float32) / DK))
    t = np.arange(S, dtype=np.float32)
    freq_row = inv_freq[(np.arange(128) % 64) % 32]
    ang = freq_row[:, None] * t[None, :]
    return np.cos(ang).astype(np.float32), np.sin(ang).astype(np.float32)


def _perm_matrix():
    P = np.zeros((64, 64), dtype=np.float32)
    for j in range(32):
        P[j, 2 * j + 1] = -1.0
    for i in range(32):
        P[32 + i, 2 * i] = 1.0
    blk = np.zeros((128, 128), dtype=np.float32)
    blk[0:64, 0:64] = P.T
    blk[64:128, 64:128] = P.T
    return blk.astype(BF)


_CACHE = {}


def _get_nc(debug=0):
    key = ("nc", debug)
    if key not in _CACHE:
        _CACHE[key] = build(debug)
    return _CACHE[key]


def make_in_maps(L0_attn_x, L0_attn_Wq, L0_attn_Wk, L0_attn_Wv, L0_attn_Wo,
                 L0_ffn_gate_w, L0_ffn_w1, L0_ffn_b1, L0_ffn_w2, L0_ffn_b2,
                 L0_ffn_gamma, L0_ffn_beta):
    x = np.ascontiguousarray(np.asarray(L0_attn_x, dtype=np.float32))
    wq = np.ascontiguousarray(np.asarray(L0_attn_Wq, dtype=np.float32))
    wk = np.ascontiguousarray(np.asarray(L0_attn_Wk, dtype=np.float32))
    wv = np.ascontiguousarray(np.asarray(L0_attn_Wv, dtype=np.float32))
    wo = np.ascontiguousarray(np.asarray(L0_attn_Wo, dtype=np.float32))
    gw = np.ascontiguousarray(np.asarray(L0_ffn_gate_w, dtype=np.float32))
    w1 = np.asarray(L0_ffn_w1, dtype=np.float32)
    b1 = np.asarray(L0_ffn_b1, dtype=np.float32)
    w2 = np.asarray(L0_ffn_w2, dtype=np.float32)
    b2 = np.ascontiguousarray(np.asarray(L0_ffn_b2, dtype=np.float32))
    gamma = np.asarray(L0_ffn_gamma, dtype=np.float32).reshape(1, D)
    beta = np.asarray(L0_ffn_beta, dtype=np.float32).reshape(1, D)

    w1p = np.ascontiguousarray(
        w1.reshape(8, 8, 128, 8, 512).transpose(0, 3, 2, 1, 4).reshape(8192, 4096))
    w2p = np.ascontiguousarray(
        w2.reshape(8, 8, 4, 128, 1024).transpose(0, 1, 3, 2, 4).reshape(8192, 4096))
    b1r = np.ascontiguousarray(b1.reshape(8, 32, 128).transpose(2, 0, 1).reshape(128, 256))
    cos_t, sin_t = _tables()
    identb = np.eye(128, dtype=BF)
    identf = np.eye(128, dtype=np.float32)
    permT = _perm_matrix()
    ones_c = np.ones((128, 1), dtype=BF)
    ones_rf = np.ones((128, 128), dtype=np.float32)

    common = dict(wq=wq, wk=wk, wv=wv, wo=wo, gate_w=gw, w1p=w1p, w2p=w2p,
                  b1r=b1r, b2=b2, gamma=gamma, beta=beta, identb=identb,
                  identf=identf, permT=permT, ones_c=ones_c, ones_rf=ones_rf)
    in_maps = []
    for c in range(8):
        b, half = c // 2, c % 2
        own = slice(half * TPC, (half + 1) * TPC)
        oth = slice((1 - half) * TPC, (2 - half) * TPC)
        xp = np.ascontiguousarray(np.concatenate([x[b, own], x[b, oth]], axis=0))
        cs = np.ascontiguousarray(np.concatenate([cos_t[:, own], cos_t[:, oth]], axis=1))
        sn = np.ascontiguousarray(np.concatenate([sin_t[:, own], sin_t[:, oth]], axis=1))
        m = dict(common)
        m.update(x=xp, cos_t=cs, sin_t=sn)
        in_maps.append(m)
    return in_maps


def kernel(L0_attn_x, L0_attn_Wq, L0_attn_Wk, L0_attn_Wv, L0_attn_Wo,
           L0_ffn_gate_w, L0_ffn_w1, L0_ffn_b1, L0_ffn_w2, L0_ffn_b2,
           L0_ffn_gamma, L0_ffn_beta):
    nc = _get_nc(0)
    in_maps = make_in_maps(L0_attn_x, L0_attn_Wq, L0_attn_Wk, L0_attn_Wv, L0_attn_Wo,
                           L0_ffn_gate_w, L0_ffn_w1, L0_ffn_b1, L0_ffn_w2, L0_ffn_b2,
                           L0_ffn_gamma, L0_ffn_beta)
    res = run_bass_kernel_spmd(nc, in_maps, core_ids=list(range(8)))
    out = np.empty((4, S, D), dtype=np.float32)
    for c in range(8):
        b, half = c // 2, c % 2
        out[b, half * TPC:(half + 1) * TPC, :] = np.asarray(
            res.results[c]["y"], dtype=np.float32)
    return out
